# version 24
# baseline (speedup 1.0000x reference)
import contextlib
import numpy as np
import ml_dtypes
import concourse.bass as bass
import concourse.mybir as mybir
from concourse.bass_utils import run_bass_kernel_spmd

F32 = mybir.dt.float32
BF16 = mybir.dt.bfloat16
AF = mybir.ActivationFunctionType
ALU = mybir.AluOpType
AX = mybir.AxisListType

NCORES = 8
NB = 4
NPOS = 512
NT = 4
D = 1024
DFF = 4096
NH = 16
HD = 64
EPS = 1e-5
VW = 65


class Flow:
    def __init__(self, nc, es):
        self.nc, self.es, self.sems, self.counts = nc, es, {}, {}

    def _get(self, name):
        if name not in self.sems:
            self.sems[name] = self.es.enter_context(self.nc.semaphore(name))
            self.counts[name] = 0
        return self.sems[name]

    def inc(self, name, instr, n=1):
        instr.then_inc(self._get(name), n)
        self.counts[name] += n
        return self.counts[name]

    def wait(self, eng, name):
        eng.wait_ge(self._get(name), self.counts[name])

    def wait_count(self, eng, name, count):
        eng.wait_ge(self._get(name), count)

    def count(self, name):
        self._get(name)
        return self.counts[name]

    def wait_lag(self, eng, name, lag):
        eng.wait_ge(self._get(name), max(0, self.counts[name] - lag))


def build_program(upto=5):
    nc = bass.Bass("TRN2", target_bir_lowering=False, detect_race_conditions=False)

    def din(name, shape, dtype=F32):
        return nc.dram_tensor(name, list(shape), dtype, kind="ExternalInput")

    x_own = din("x_own", [NB * NPOS, D])
    x_hl = din("x_hl", [NB * 128, D])
    x_hr = din("x_hr", [NB * 128, D])
    maskv_in = din("maskv", [128, 6])
    qkv_wT_in = din("qkv_wT", [D, 3 * D])
    ln1_w_in = din("ln1_w_fm", [128, 8])
    ln1_b_in = din("ln1_b_fm", [128, 8])
    qkv_b_in = din("qkv_b_row", [1, 3 * D])
    proj_wT_in = din("proj_wT", [D, D])
    proj_b_in = din("proj_b_row", [1, D])
    ln2_w_in = din("ln2_w_fm", [128, 8])
    ln2_b_in = din("ln2_b_fm", [128, 8])
    fc1_wT_in = din("fc1_wT", [D, DFF])
    fc1_b_in = din("fc1_b_row", [1, DFF])
    fc2_wT_in = din("fc2_wT", [DFF, D])
    fc2_b_in = din("fc2_b_row", [1, D])
    WqT_in = din("WqT", [D, D])
    WkT_in = din("WkT", [D, D])
    WvT_in = din("WvT", [D, D])
    bq_in = din("bq_row", [1, D])
    bk_in = din("bk_row", [1, D])
    bv_in = din("bv_row", [1, D])
    Win_in = din("Win", [D, 64])
    bin_in = din("bin_fm", [64, 1])
    W1k_in = din("W1kron", [64, D])
    b1_in = din("b1_fm", [128, 8])
    W2bT_in = din("W2blkT", [D, 64])
    b2_in = din("b2_fm", [64, 1])
    Wout_in = din("Wout", [64, D])
    bout_in = din("bout_row", [1, D])

    out = nc.dram_tensor("out", [NB * NPOS, D], F32, kind="ExternalOutput")
    fc1bf = nc.dram_tensor("fc1bf", [D, DFF], BF16, kind="Internal")

    es = contextlib.ExitStack()
    f = Flow(nc, es)
    bar = nc.all_engine_barrier

    def dma(eng, dst, src, q="ld", **kw):
        ins = eng.dma_start(dst, src, **kw)
        f.inc("dma_" + q, ins, 16)
        return ins

    def dwait(eng, q="ld"):
        f.wait(eng, "dma_" + q)

    with es:
        def sb(name, shape, dtype):
            return es.enter_context(nc.sbuf_tensor(name, list(shape), dtype))

        def psum(name, shape, dtype=F32):
            return es.enter_context(nc.psum_tensor(name, list(shape), dtype))

        R = sb("R", [128, NB * NT * D], F32)
        ident = sb("ident", [128, 128], BF16)
        identF = sb("identF", [128, 128], F32)
        ones_all = sb("ones_all", [128, 128], BF16)
        scrA = sb("scrA", [128, D], F32)
        st = sb("st", [128, 64], F32)
        maskv = sb("maskv_sb", [128, 6], F32)
        rows1 = sb("rows1", [128, D], BF16)
        rows2 = sb("rows2", [128, D], BF16)
        bqk_fm = sb("bqk_fm", [128, 16], F32)
        bf1_fm = sb("bf1_fm", [128, 32], F32)
        lnw = sb("lnw", [128, 32], F32)
        lnbf = sb("lnbf", [128, 16], BF16)
        b1fm = sb("b1fm", [128, 8], F32)
        bin_sb = sb("bin_sb", [64, 1], F32)
        b2_sb = sb("b2_sb", [64, 1], F32)

        P0 = psum("P0", [128, 512])
        P1 = psum("P1", [128, 512])
        P2 = psum("P2", [128, 512])
        P3 = psum("P3", [128, 512])
        P4 = psum("P4", [128, 512])
        Psim = psum("Psim", [128, 384])
        Psim2 = psum("Psim2", [128, 512])
        Ptr = psum("Ptr", [128, 1024], BF16)

        TT, VV, SS, GG = nc.tensor, nc.scalar and nc.vector, nc.scalar, nc.gpsimd
        TT, VV, SS, GG = nc.tensor, nc.vector, nc.scalar, nc.gpsimd
        negmax = st[:, 32:48]
        rcp = st[:, 48:64]

        def ln_stats_apply(x_ap, out_ap, gout):
            ln_group_stats([x_ap], 0, gout)
            return ln_apply(0, x_ap, out_ap)

        def ln_group_stats(x_aps, base, gout, ss_sum=False):
            n = len(x_aps)
            for i, xap in enumerate(x_aps):
                f.inc("a2v", SS.activation(gout, xap, AF.Square,
                                           accum_out=st[:, 4 + i:5 + i]))
                if ss_sum:
                    f.inc("a2v", SS.activation(gout, xap, AF.Identity,
                                               accum_out=st[:, 24 + base + i:25 + base + i]))
                else:
                    f.inc("vv", VV.tensor_scalar(scrA[:], xap, 1.0, 0.0, ALU.mult, ALU.add,
                                                 accum_out=st[:, 24 + base + i:25 + base + i]))
            if not ss_sum:
                f.wait(VV, "vv")
            f.wait(VV, "a2v")
            f.inc("vv", VV.tensor_scalar(st[:, 24 + base:24 + base + n],
                                         st[:, 24 + base:24 + base + n],
                                         1.0 / D, None, ALU.mult))
            f.inc("vv", VV.tensor_scalar(st[:, 4:4 + n], st[:, 4:4 + n], 1.0 / D, EPS,
                                         ALU.mult, ALU.add))
            f.wait(VV, "vv")
            f.inc("vv", VV.scalar_tensor_tensor(st[:, 0:n], st[:, 24 + base:24 + base + n],
                                                1.0, st[:, 24 + base:24 + base + n],
                                                ALU.mult, ALU.mult))
            f.wait(VV, "vv")
            f.inc("vv", VV.tensor_sub(st[:, 4:4 + n], st[:, 4:4 + n], st[:, 0:n]))
            f.wait(VV, "vv")
            f.inc("v2a", VV.reciprocal(st[:, 0:n], st[:, 4:4 + n]))
            f.wait(SS, "v2a")
            f.inc("a2v", SS.activation(st[:, 8 + base:8 + base + n], st[:, 0:n], AF.Sqrt))
            f.wait(VV, "a2v")
            f.inc("v2a", VV.scalar_tensor_tensor(st[:, 24 + base:24 + base + n],
                                                 st[:, 24 + base:24 + base + n], -1.0,
                                                 st[:, 8 + base:8 + base + n],
                                                 ALU.mult, ALU.mult))
            f.wait(SS, "v2a")

        def ln_apply(k, x_ap, out_ap):
            return SS.activation(out_ap, x_ap, AF.Identity,
                                 bias=st[:, 24 + k:25 + k], scale=st[:, 8 + k:9 + k])

        def transpose8(src_sb, dst_sb, dst_off, dst_stride, row_len):
            ins = None
            for c in range(8):
                ins = TT.transpose(Ptr[:, c * 128:(c + 1) * 128],
                                   src_sb[:, c * 128:(c + 1) * 128], ident[:])
            f.inc("t2v", ins)
            f.wait(VV, "t2v")
            dst = bass.AP(dst_sb, dst_off, [[row_len, 128], [dst_stride, 8], [1, 128]])
            return VV.tensor_copy(dst, Ptr[:])

        def row_to_fm(src_row_ap_fn, n_chunks, pcol0, evac_dst, base_p=0):
            ins = None
            for i in range(n_chunks):
                ins = TT.transpose(P4[:, pcol0 + i:pcol0 + i + 1], src_row_ap_fn(i),
                                   identF[base_p:base_p + 1, base_p:base_p + 1])
            f.inc("t2v", ins)
            f.wait(VV, "t2v")
            return VV.tensor_copy(evac_dst, P4[:, pcol0:pcol0 + n_chunks])

        GG.memset(ident[:], 0.0)
        GG.memset(identF[:], 0.0)
        GG.memset(ones_all[:], 1.0)
        bar()
        GG.affine_select(out=ident[:], in_=ident[:], compare_op=ALU.not_equal,
                         fill=1.0, base=0, pattern=[[-1, 128]], channel_multiplier=1)
        GG.affine_select(out=identF[:], in_=identF[:], compare_op=ALU.not_equal,
                         fill=1.0, base=0, pattern=[[-1, 128]], channel_multiplier=1)
        bar()

        dma(nc.sync, maskv[:], maskv_in[:])
        dma(nc.sync, lnw[:, 0:8], ln1_w_in[:])
        dma(nc.sync, lnw[:, 8:16], ln1_b_in[:])
        dma(nc.sync, lnw[:, 16:24], ln2_w_in[:])
        dma(nc.sync, lnw[:, 24:32], ln2_b_in[:])
        dma(nc.sync, b1fm[:], b1_in[:])
        dma(nc.sync, bin_sb[:], bin_in[:])
        dma(nc.sync, b2_sb[:], b2_in[:])
        dma(nc.gpsimd, rows1[64:65, :], fc2_b_in[:], q="g")
        dma(nc.gpsimd, rows2[0:1, :], bq_in[:], q="g")
        dma(nc.gpsimd, rows2[32:33, :], bk_in[:], q="g")
        dma(nc.gpsimd, rows2[64:65, :], bout_in[:], q="g")

        dma(nc.sync, R[0:1, 0:3 * D], qkv_b_in[:])
        dma(nc.sync, R[32:33, 0:D], proj_b_in[:])
        dma(nc.sync, R[32:33, 4096:4096 + DFF], fc1_b_in[:])
        dwait(TT, "g")
        dwait(VV, "g")
        dwait(SS, "g")
        dwait(TT)
        dwait(VV)
        dwait(SS)
        bar()
        VV.tensor_copy(lnbf[:, 0:8], lnw[:, 8:16])
        VV.tensor_copy(lnbf[:, 8:16], lnw[:, 24:32])
        bar()
        with nc.sbuf_tensor("fc1all", [128, 8 * DFF], BF16) as fc1all:
            for c in range(8):
                dma(nc.gpsimd, fc1all[:, c * DFF:(c + 1) * DFF],
                    fc1_wT_in[c * 128:(c + 1) * 128, :], q="g")
            dwait(TT, "g")
            bar()
            for s in range(8):
                for c in range(8):
                    TT.matmul(P0[0:1, :], lnbf[:, 8 + c:9 + c],
                              fc1all[:, c * DFF + s * 512:c * DFF + (s + 1) * 512],
                              start=(c == 0), stop=(c == 7))
                bar()
                VV.scalar_tensor_tensor(
                    R[64:65, 4096 + s * 512:4096 + (s + 1) * 512], P0[0:1, :], 1.0,
                    R[32:33, 4096 + s * 512:4096 + (s + 1) * 512], ALU.mult, ALU.add)
                bar()
            row_to_fm(lambda i: R[64:65, 4096 + i * 128:4096 + (i + 1) * 128], 32, 0,
                      bf1_fm[:], base_p=64)
            bar()
            for c in range(8):
                VV.tensor_scalar(fc1all[:, c * DFF:(c + 1) * DFF],
                                 fc1all[:, c * DFF:(c + 1) * DFF],
                                 lnw[:, 16 + c:17 + c], None, ALU.mult)
            bar()
            for c in range(8):
                dma(nc.sync, fc1bf[c * 128:(c + 1) * 128, :],
                    fc1all[:, c * DFF:(c + 1) * DFF], q="pst")
            dwait(nc.sync, "pst")
            bar()


        att_es = contextlib.ExitStack()
        qkvW = att_es.enter_context(nc.sbuf_tensor("qkvW", [128, 8 * 3072], BF16))
        projW = att_es.enter_context(nc.sbuf_tensor("projW", [128, 8 * D], BF16))
        for c in range(8):
            dma(nc.gpsimd, qkvW[:, c * 3072:(c + 1) * 3072],
                qkv_wT_in[c * 128:(c + 1) * 128, :], q="g")
            dma(nc.gpsimd, projW[:, c * D:(c + 1) * D],
                proj_wT_in[c * 128:(c + 1) * 128, :], q="g")
        dwait(TT, "g")
        bar()


        for s in range(6):
            for c in range(8):
                TT.matmul(P0[0:1, :], lnbf[:, c:c + 1],
                          qkvW[:, c * 3072 + s * 512:c * 3072 + (s + 1) * 512],
                          start=(c == 0), stop=(c == 7))
            bar()
            VV.scalar_tensor_tensor(R[0:1, 4096 + s * 512:4096 + (s + 1) * 512],
                                    P0[0:1, :], 1.0, R[0:1, s * 512:(s + 1) * 512],
                                    ALU.mult, ALU.add)
            bar()
        VV.tensor_scalar(R[0:1, 4096:4096 + D], R[0:1, 4096:4096 + D], 0.125, None,
                         ALU.mult)
        bar()
        for c in range(8):
            VV.tensor_scalar(qkvW[:, c * 3072:c * 3072 + D],
                             qkvW[:, c * 3072:c * 3072 + D],
                             lnw[:, c:c + 1], 0.125, ALU.mult, ALU.mult)
            VV.tensor_scalar(qkvW[:, c * 3072 + D:(c + 1) * 3072],
                             qkvW[:, c * 3072 + D:(c + 1) * 3072],
                             lnw[:, c:c + 1], None, ALU.mult)
        bar()
        row_to_fm(lambda i: R[0:1, 4096 + i * 128:4096 + (i + 1) * 128], 16, 0,
                  bqk_fm[:])
        VV.tensor_copy(rows1[0:1, :], R[0:1, 4096 + 2 * D:4096 + 3 * D])
        bar()
        VV.tensor_copy(rows1[32:33, :], R[32:33, 0:D])
        bar()

        if upto == 0:
            dbg1 = nc.dram_tensor("dbg1", [128, 64], F32, kind="ExternalOutput")
            dbg2 = nc.dram_tensor("dbg2", [128, 3072], BF16, kind="ExternalOutput")
            dbg3 = nc.dram_tensor("dbg3", [128, 2 * D], BF16, kind="ExternalOutput")
            bar()
            dma(nc.sync, dbg1[:, 0:16], bqk_fm[:], q="st")
            dma(nc.sync, dbg1[:, 16:48], bf1_fm[:], q="st")
            dma(nc.sync, dbg1[:, 48:56], lnw[:, 0:8], q="st")
            dma(nc.sync, dbg2[:], qkvW[:, 0:3072], q="st")
            dma(nc.sync, dbg3[:, 0:D], rows1[:], q="st")
            dma(nc.sync, dbg3[:, D:2 * D], rows2[:], q="st")
        if upto >= 1 or upto == 11:
            s1 = contextlib.ExitStack()
            with s1:
                def sb1(name, shape, dtype):
                    return s1.enter_context(nc.sbuf_tensor(name, list(shape), dtype))

                xcur = sb1("xcur", [128, 2 * D], F32)
                xh0 = sb1("xh0", [128, 2 * D], F32)
                a1 = sb1("a1", [128, D], BF16)
                a1T = sb1("a1T", [128, 8 * 768], BF16)
                qfm = sb1("qfm", [128, 8 * 512], BF16)
                kfm = sb1("kfm", [128, 8 * 768], BF16)
                v_sb = sb1("v_sb", [128, 6 * NH * VW], BF16)
                o_tok = sb1("o_tok", [128, D], BF16)
                oT = sb1("oT", [128, D], BF16)
                p_sb = sb1("p_sb", [128, 2 * 384], BF16)
                pt_sb = sb1("pt_sb", [128, 2 * 384], BF16)
                VROW = 6 * NH * VW
                xfree_c = [0, 0]

                for j in range(6):
                    dst = bass.AP(v_sb, j * NH * VW + HD, [[VROW, 128], [VW, NH]])
                    src = bass.AP(maskv, j, [[6, 128], [0, NH]])
                    VV.tensor_copy(dst, src)
                bar()

                ldx = {}
                a1T_done = {}
                last_ptr_evac = [("ptc", f.count("ptc"))]
                f.count("ptcS")
                p01_free = [("v2t", f.count("v2t"))]
                ln_p01 = [("v2t", f.count("v2t"))]

                def xsrc(b, j):
                    if j == 0:
                        return x_hl[b * 128:(b + 1) * 128, :]
                    if j == 5:
                        return x_hr[b * 128:(b + 1) * 128, :]
                    return x_own[b * NPOS + (j - 1) * 128:b * NPOS + j * 128, :]

                def ln_tile(b, j):
                    g = b * 6 + j
                    f.wait(VV, "a2v")
                    nb_, nj = (b, j + 1) if j + 1 < 6 else (b + 1, 0)
                    if nb_ < NB:
                        f.wait_count(nc.sync, "a2v", xfree_c[(g + 1) % 2])
                        dma(nc.sync, xcur[:, ((g + 1) % 2) * D:((g + 1) % 2 + 1) * D],
                            xsrc(nb_, nj))
                        ldx[g + 1] = f.count("dma_ld")
                    xc = xcur[:, (g % 2) * D:(g % 2 + 1) * D]
                    xd = xh0[:, (g % 2) * D:(g % 2 + 1) * D]
                    gact = xh0[:, ((g + 1) % 2) * D:((g + 1) % 2 + 1) * D]
                    f.wait_count(VV, "dma_ld", ldx[g])
                    f.wait_count(SS, "dma_ld", ldx[g])
                    ln_group_stats([xc], 0, gact)
                    f.inc("a2v", ln_apply(0, xc, xd))
                    xfree_c[g % 2] = f.count("a2v")
                    f.wait(VV, "a2v")
                    ln_group_stats([xd], 0, gact)
                    f.wait(SS, "t2v")
                    f.inc("a2v", ln_apply(0, xd, a1[:]))
                    f.wait(TT, "a2v")
                    f.wait_count(TT, p01_free[0][0], p01_free[0][1])
                    P0b = P0[:].bitcast(BF16)
                    ins = None
                    for c in range(8):
                        ins = TT.transpose(P0b[:, c * 128:(c + 1) * 128],
                                           a1[:, c * 128:(c + 1) * 128], ident[:])
                    f.inc("t2v", ins)
                    f.wait(VV, "t2v")
                    dst = bass.AP(a1T, j * 128, [[8 * 768, 128], [768, 8], [1, 128]])
                    cnt = f.inc("v2t", VV.tensor_copy(dst, P0b[:]))
                    ln_p01[0] = ("v2t", cnt)
                    if j == 5:
                        a1T_done[b] = cnt

                def qkv_phase(b):
                    f.wait_count(TT, "v2t", a1T_done[b])
                    f.wait_count(TT, ln_p01[0][0], ln_p01[0][1])
                    f.wait(TT, "a2t")
                    for m in range(8):
                        pq = (P0, P1)[m % 2]
                        if m:
                            f.wait_lag(TT, "a2t", 1)
                        ins = None
                        for c in range(8):
                            ins = TT.matmul(pq[:],
                                            qkvW[:, c * 3072 + m * 128:c * 3072 + (m + 1) * 128],
                                            a1T[:, c * 768 + 128:c * 768 + 640],
                                            start=(c == 0), stop=(c == 7))
                        f.inc("t2a", ins)
                        f.wait(SS, "t2a")
                        f.inc("a2t", SS.activation(qfm[:, m * 512:(m + 1) * 512], pq[:],
                                                   AF.Identity, bias=bqk_fm[:, m:m + 1]))
                    f.wait(TT, "a2t")
                    for m in range(8):
                        ka, kb = ((P0, P1), (P2, P3))[m % 2]
                        if m:
                            f.wait_lag(TT, "a2t", 1)
                        ins = None
                        for (off, w_, pk) in ((0, 512, ka), (512, 256, kb)):
                            for c in range(8):
                                ins = TT.matmul(
                                    pk[:, 0:w_],
                                    qkvW[:, c * 3072 + D + m * 128:c * 3072 + D + (m + 1) * 128],
                                    a1T[:, c * 768 + off:c * 768 + off + w_],
                                    start=(c == 0), stop=(c == 7))
                        f.inc("t2a", ins)
                        f.wait(SS, "t2a")
                        SS.activation(kfm[:, m * 768:m * 768 + 512], ka[:],
                                      AF.Identity, bias=bqk_fm[:, 8 + m:9 + m])
                        f.inc("a2t", SS.activation(kfm[:, m * 768 + 512:(m + 1) * 768],
                                                   kb[:, 0:256], AF.Identity,
                                                   bias=bqk_fm[:, 8 + m:9 + m]))
                    f.wait(TT, "a2t")
                    for j in range(6):
                        va, vb = ((P2, P3), (P0, P1))[j % 2]
                        if j:
                            f.wait_lag(TT, "v2t", 1)
                        ins = None
                        for (s, pv) in ((0, va), (1, vb)):
                            for c in range(8):
                                TT.matmul(
                                    pv[:],
                                    a1T[:, c * 768 + j * 128:c * 768 + (j + 1) * 128],
                                    qkvW[:, c * 3072 + 2 * D + s * 512:c * 3072 + 2 * D + (s + 1) * 512],
                                    start=(c == 0), stop=False)
                            ins = TT.matmul(pv[:], ones_all[0:1, :],
                                            rows1[0:1, s * 512:(s + 1) * 512],
                                            start=False, stop=True)
                        f.inc("t2v", ins)
                        f.wait(VV, "t2v")
                        base_ = j * NH * VW
                        d0 = bass.AP(v_sb, base_, [[VROW, 128], [VW, 8], [1, HD]])
                        d1 = bass.AP(v_sb, base_ + 8 * VW, [[VROW, 128], [VW, 8], [1, HD]])
                        VV.tensor_scalar(d0, va[:], maskv[:, j:j + 1], None, ALU.mult)
                        ins = VV.tensor_scalar(d1, vb[:], maskv[:, j:j + 1], None,
                                               ALU.mult)
                        f.inc("v2t", ins)
                    f.wait(TT, "v2t")

                def attention_batch(b):
                    f.wait(TT, "a2t")
                    heads = [(w, h) for w in range(NT) for h in range(NH)]
                    nhd = len(heads)
                    simc, Tc, copyc, expc, maxc, pvc = {}, {}, {}, {}, {}, {}
                    normc, projevc = {}, {}

                    def po_slot(h):
                        return ((P2, h * VW) if h < 7 else
                                (P3, (h - 7) * VW) if h < 14 else
                                (P4, (h - 14) * VW))

                    def emit_sim(i):
                        w, h = heads[i]
                        S = (Psim[:], Psim2[:, 0:384])[i % 2]
                        if i >= 2:
                            f.wait_count(TT, "a2t", expc[i - 2])
                        hp = (h % 2) * 64
                        hc = h // 2
                        simc[i] = f.inc("t2v", TT.matmul(
                            S,
                            qfm[hp:hp + 64, hc * 512 + w * 128:hc * 512 + (w + 1) * 128],
                            kfm[hp:hp + 64, hc * 768 + w * 128:hc * 768 + w * 128 + 384],
                            start=True, stop=True))
                        f.wait_count(VV, "t2v", simc[i])
                        maxc[i] = f.inc("v2a", VV.reduce_max(
                            negmax[:, h:h + 1], S, AX.X, negate=True))
                        f.wait_count(SS, "v2a", maxc[i])
                        if i >= 2:
                            f.wait_count(SS, "t2v", Tc[i - 2])
                        expc[i] = f.inc("a2t", SS.activation(
                            p_sb[:, (i % 2) * 384:(i % 2) * 384 + 384], S, AF.Exp,
                            bias=negmax[:, h:h + 1]))

                    def emit_T(i):
                        f.wait_count(TT, "a2t", expc[i])
                        f.wait_count(TT, last_ptr_evac[0][0], last_ptr_evac[0][1])
                        ins = None
                        for kc in range(3):
                            ins = TT.transpose(
                                Ptr[:, kc * 128:(kc + 1) * 128],
                                p_sb[:, (i % 2) * 384 + kc * 128:(i % 2) * 384 + (kc + 1) * 128],
                                ident[:])
                        Tc[i] = f.inc("t2v", ins)
                        dst = pt_sb[:, (i % 2) * 384:(i % 2) * 384 + 384]
                        if i % 2 == 0:
                            f.wait_count(VV, "t2v", Tc[i])
                            if i >= 2:
                                f.wait_count(VV, "t2v", pvc[i - 2])
                            copyc[i] = ("ptc", f.inc("ptc", VV.tensor_copy(dst, Ptr[:, 0:384])))
                        else:
                            f.wait_count(SS, "t2v", Tc[i])
                            if i >= 2:
                                f.wait_count(SS, "t2v", pvc[i - 2])
                            copyc[i] = ("ptcS", f.inc("ptcS", SS.activation(
                                dst, Ptr[:, 0:384], AF.Identity)))
                        last_ptr_evac[0] = copyc[i]

                    def emit_PV(i):
                        w, h = heads[i]
                        f.wait_count(TT, copyc[i][0], copyc[i][1])
                        if h == 0 and w >= 1:
                            f.wait_count(TT, "v2t", normc[w - 1])
                        po, oc = po_slot(h)
                        ins = None
                        for kc in range(3):
                            ins = TT.matmul(
                                po[:, oc:oc + VW],
                                pt_sb[:, (i % 2) * 384 + kc * 128:(i % 2) * 384 + (kc + 1) * 128],
                                v_sb[:, (w + kc) * NH * VW + h * VW:(w + kc) * NH * VW + (h + 1) * VW],
                                start=(kc == 0), stop=(kc == 2))
                        pvc[i] = f.inc("t2v", ins)

                    def emit_norm_part(w, part):
                        lo, hi, bank = ((0, 7, P2), (7, 14, P3), (14, 16, P4))[part]
                        f.wait_count(VV, "t2v", pvc[w * NH + hi - 1])
                        f.inc("vv", VV.reciprocal(
                            rcp[:, lo:hi], bass.AP(bank, HD, [[512, 128], [VW, hi - lo]])))
                        f.wait(VV, "vv")
                        ins = None
                        for h in range(lo, hi):
                            po, oc = po_slot(h)
                            ins = VV.tensor_scalar(o_tok[:, h * HD:(h + 1) * HD],
                                                   po[:, oc:oc + HD], rcp[:, h:h + 1],
                                                   None, ALU.mult)
                        cnt = f.inc("v2t", ins)
                        ch = ((0, 3), (3, 7), (7, 8))[part]
                        f.wait_count(TT, "v2t", cnt)
                        f.wait_count(TT, last_ptr_evac[0][0], last_ptr_evac[0][1])
                        ins = None
                        for c in range(ch[0], ch[1]):
                            ins = TT.transpose(Ptr[:, c * 128:(c + 1) * 128],
                                               o_tok[:, c * 128:(c + 1) * 128], ident[:])
                        f.inc("t2v", ins)
                        f.wait(VV, "t2v")
                        dst = bass.AP(oT, ch[0] * 128,
                                      [[D, 128], [128, ch[1] - ch[0]], [1, 128]])
                        cnt2 = f.inc("v2t", VV.tensor_copy(
                            dst, Ptr[:, ch[0] * 128:ch[1] * 128]))
                        last_ptr_evac[0] = ("v2t", cnt2)
                        if part == 2:
                            normc[w] = cnt2

                    def emit_proj(w):
                        f.wait_count(TT, "v2t", normc[w])
                        if w >= 1:
                            f.wait_count(TT, "v2t", projevc[w - 1])
                        f.wait_count(TT, ln_p01[0][0], ln_p01[0][1])
                        ins = None
                        for (s, pp) in ((0, P0), (1, P1)):
                            for c in range(8):
                                TT.matmul(pp[:], oT[:, c * 128:(c + 1) * 128],
                                          projW[:, c * D + s * 512:c * D + (s + 1) * 512],
                                          start=(c == 0), stop=False)
                            ins = TT.matmul(pp[:], ones_all[32:33, :],
                                            rows1[32:33, s * 512:(s + 1) * 512],
                                            start=False, stop=True)
                        prj = f.inc("t2a", ins)
                        f.wait_count(SS, "t2a", prj)
                        slot = (b * NT + w) * D
                        SS.activation(R[:, slot:slot + 512], P0[:], AF.Identity)
                        ins = SS.activation(R[:, slot + 512:slot + D], P1[:], AF.Identity)
                        projevc[w] = f.inc("v2t", ins)
                        p01_free[0] = ("v2t", projevc[w])

                    pending_proj = []
                    ln_tasks = [(b + 1, j) for j in range(6)] if b + 1 < NB else []
                    for i in range(nhd + 2):
                        if i < nhd:
                            emit_sim(i)
                        if 1 <= i <= nhd:
                            emit_T(i - 1)
                        if 2 <= i <= nhd + 1:
                            emit_PV(i - 2)
                            w2, h2_ = heads[i - 2]
                            if h2_ == 6:
                                emit_norm_part(w2, 0)
                            elif h2_ == 13:
                                emit_norm_part(w2, 1)
                            elif h2_ == NH - 1:
                                emit_norm_part(w2, 2)
                                pending_proj.append(w2)
                        if pending_proj and (i >= nhd + 1 or
                                             (i - 2 >= 0 and i - 2 - pending_proj[0] * NH >= 21)):
                            wp = pending_proj.pop(0)
                            emit_proj(wp)
                            if ln_tasks:
                                bb, jj = ln_tasks.pop(0)
                                ln_tile(bb, jj)
                    while pending_proj:
                        emit_proj(pending_proj.pop(0))
                    while ln_tasks:
                        bb, jj = ln_tasks.pop(0)
                        ln_tile(bb, jj)
                    f.wait(TT, "v2t")
                    f.wait(TT, "ptc")
                    f.wait(TT, "ptcS")

                dma(nc.sync, xcur[:, 0:D], xsrc(0, 0))
                ldx[0] = f.count("dma_ld")
                for j in range(6):
                    ln_tile(0, j)
                for b in range(NB):
                    qkv_phase(b)
                    attention_batch(b)
                bar()

        att_es.close()

        if upto >= 2:
            s2 = contextlib.ExitStack()
            with s2:
                def sb2(name, shape, dtype):
                    return s2.enter_context(nc.sbuf_tensor(name, list(shape), dtype))

                fc2W = sb2("fc2W", [128, 32 * D], BF16)
                fc1s = sb2("fc1s", [128, 2 * D], BF16)
                x2T = sb2("x2T", [128, 8 * 512], BF16)
                yb = sb2("yb", [128, 32 * 512], BF16)
                xall = sb2("xall", [128, 2 * D], F32)
                xs = sb2("xs", [128, 2 * D], F32)
                h0b = sb2("h0b", [128, D], F32)
                h2 = sb2("h2", [128, D], BF16)

                bar()

                xsf = [0, 0]
                h0f = [0, 0]

                def h0_phase(b):
                    for t in range(NT):
                        g = b * NT + t
                        f.wait_count(nc.sync, "a2v", h0f[g % 2])
                        dma(nc.sync, xall[:, (g % 2) * D:(g % 2 + 1) * D],
                            x_own[b * NPOS + t * 128:b * NPOS + (t + 1) * 128, :])
                        ldg = f.count("dma_ld")
                        f.wait_count(VV, "dma_ld", ldg)
                        f.wait_count(SS, "dma_ld", ldg)
                        xc = xall[:, (g % 2) * D:(g % 2 + 1) * D]
                        f.wait(VV, "a2v")
                        ln_group_stats([xc], 4, h2[:])
                        if t or b:
                            f.wait(SS, "vv")
                        f.inc("a2v", ln_apply(4, xc, h0b[:]))
                        h0f[g % 2] = f.count("a2v")
                        f.wait(VV, "a2v")
                        slot = g * D
                        f.inc("vv", VV.tensor_add(R[:, slot:slot + D],
                                                  R[:, slot:slot + D], h0b[:]))

                for b in range(NB):
                    if b == 0:
                        h0_phase(0)
                    f.wait(SS, "vv")
                    f.wait(VV, "vv")
                    ln_group_stats([R[:, (b * NT + t) * D:(b * NT + t + 1) * D]
                                    for t in range(NT)], 0, yb[:, 0:1024])
                    for t in range(NT):
                        slot = (b * NT + t) * D
                        f.wait(SS, "t2v")
                        f.inc("a2v", ln_apply(t, R[:, slot:slot + D], h2[:]))
                        f.wait(TT, "a2v")
                        f.inc("v2t", transpose8(h2, x2T, t * 128, 512, 8 * 512))
                        f.wait(TT, "v2t")
                    x2T_done = f.count("v2t")
                    ldc = {}
                    t2a_c = {}
                    for m in range(2):
                        dma(nc.sync, fc1s[:, (m % 2) * D:((m % 2) + 1) * D],
                            bass.AP(fc1bf, m * 128, [[DFF, 128], [128 * DFF, 8], [1, 128]]),
                            q="f1")
                        ldc[m] = f.count("dma_f1")
                    f.wait_count(TT, "v2t", x2T_done)
                    for m in range(32):
                        if b == 0:
                            dma(nc.gpsimd, fc2W[:, m * D:(m + 1) * D],
                                fc2_wT_in[m * 128:(m + 1) * 128, :], q="g")
                            if m == 31:
                                fc2w_ld = f.count("dma_g")
                        f.wait_count(TT, "dma_f1", ldc[m])
                        pq = (P0, P1, P2, P3)[m % 4]
                        f.wait_lag(TT, "a2t", 3)
                        ins = None
                        for c in range(8):
                            ins = TT.matmul(pq[:],
                                            fc1s[:, (m % 2) * D + c * 128:(m % 2) * D + (c + 1) * 128],
                                            x2T[:, c * 512:(c + 1) * 512],
                                            start=(c == 0), stop=(c == 7))
                        f.inc("t2a", ins)
                        t2a_c[m] = f.count("t2a")
                        if m + 2 < 32:
                            f.wait_count(nc.sync, "t2a", t2a_c[m])
                            dma(nc.sync, fc1s[:, ((m + 2) % 2) * D:(((m + 2) % 2) + 1) * D],
                                bass.AP(fc1bf, (m + 2) * 128,
                                        [[DFF, 128], [128 * DFF, 8], [1, 128]]),
                                q="f1")
                            ldc[m + 2] = f.count("dma_f1")
                        f.wait(SS, "t2a")
                        f.inc("a2t", SS.activation(yb[:, m * 512:(m + 1) * 512], pq[:],
                                                   AF.Gelu, bias=bf1_fm[:, m:m + 1]))
                    if b + 1 < NB:
                        h0_phase(b + 1)
                    for t in range(NT):
                        slot = (b * NT + t) * D
                        if t == 0:
                            f.wait(TT, "a2t")
                        if b == 0 and t == 0:
                            f.wait_count(TT, "dma_g", fc2w_ld)
                        fa, fb = ((P4, Psim2), (P0, P1))[t % 2]
                        f.wait_lag(nc.sync, "xs2", 1)
                        dma(nc.sync, xs[:, (t % 2) * D:(t % 2 + 1) * D],
                            x_own[b * NPOS + t * 128:b * NPOS + (t + 1) * 128, :])
                        xld = f.count("dma_ld")
                        f.wait_lag(TT, "v2t", 1)
                        ins = None
                        for (s, pp) in ((0, fa), (1, fb)):
                            for Y in range(32):
                                TT.matmul(pp[:],
                                          yb[:, Y * 512 + t * 128:Y * 512 + (t + 1) * 128],
                                          fc2W[:, Y * D + s * 512:Y * D + (s + 1) * 512],
                                          start=(Y == 0), stop=False)
                            ins = TT.matmul(pp[:], ones_all[64:65, :],
                                            rows1[64:65, s * 512:(s + 1) * 512],
                                            start=False, stop=True)
                        f.inc("t2v", ins)
                        f.wait_count(VV, "dma_ld", xld)
                        f.wait(VV, "t2v")
                        f.inc("xs2", VV.tensor_add(R[:, slot:slot + D],
                                                   R[:, slot:slot + D],
                                                   xs[:, (t % 2) * D:(t % 2 + 1) * D]))
                        f.wait(VV, "xs2")
                        VV.scalar_tensor_tensor(R[:, slot:slot + 512], fa[:], 1.0,
                                                R[:, slot:slot + 512], ALU.mult, ALU.add)
                        ins = VV.scalar_tensor_tensor(R[:, slot + 512:slot + D],
                                                      fb[:, 0:512],
                                                      1.0, R[:, slot + 512:slot + D],
                                                      ALU.mult, ALU.add)
                        f.inc("v2t", ins)
                    bar()

        def emit_ffn(tag, prefetch=None):
            s3 = contextlib.ExitStack()
            with s3:
                def sb3(name, shape, dtype):
                    return s3.enter_context(nc.sbuf_tensor(name, list(shape), dtype))

                Win_sb = sb3("Win_sb" + tag, [128, 8 * 64], BF16)
                W1k_sb = sb3("W1k_sb" + tag, [64, D], BF16)
                W2bT_sb = sb3("W2bT_sb" + tag, [128, 8 * 64], BF16)
                Wout_sb = sb3("Wout_sb" + tag, [64, D], BF16)
                h1 = sb3("h1" + tag, [128, D], BF16)
                h1T = sb3("h1T" + tag, [128, 2 * 8 * 512], BF16)
                s_sb = sb3("s_sb" + tag, [64, 512], BF16)
                hh = sb3("hh" + tag, [128, 8 * 512], BF16)
                g_sb = sb3("g_sb" + tag, [64, 512], BF16)

                dma(nc.gpsimd, Win_sb[:],
                    bass.AP(Win_in, 0, [[64, 128], [8192, 8], [1, 64]]), q="g")
                dma(nc.gpsimd, W1k_sb[:], W1k_in[:], q="g")
                dma(nc.gpsimd, W2bT_sb[:],
                    bass.AP(W2bT_in, 0, [[64, 128], [8192, 8], [1, 64]]), q="g")
                dma(nc.gpsimd, Wout_sb[:], Wout_in[:], q="g")
                if prefetch:
                    prefetch()
                h1T_done = {}

                def ln_ffn(b):
                    p3 = (b % 2) * 4096
                    f.wait(VV, "a2v")
                    f.wait(SS, "t2v")
                    ln_group_stats([R[:, (b * NT + t) * D:(b * NT + t + 1) * D]
                                    for t in range(NT)], (b % 2) * 4, h1[:])
                    cnt = None
                    for t in range(NT):
                        slot = (b * NT + t) * D
                        f.wait(SS, "t2v")
                        f.inc("a2v", ln_apply((b % 2) * 4 + t, R[:, slot:slot + D],
                                              h1[:]))
                        f.wait(TT, "a2v")
                        if cnt is not None:
                            f.wait_count(TT, "v2t", cnt)
                        cnt = f.inc("v2t", transpose8(h1, h1T, p3 + t * 128, 512,
                                                      2 * 8 * 512))
                    h1T_done[b] = cnt

                ln_ffn(0)
                dwait(TT, "g")
                for b in range(NB):
                    p3 = (b % 2) * 4096
                    f.wait(TT, "v2t")
                    f.wait(TT, "a2t")
                    ins = None
                    for c in range(8):
                        ins = TT.matmul(P0[0:64, :], Win_sb[:, c * 64:(c + 1) * 64],
                                        h1T[:, p3 + c * 512:p3 + (c + 1) * 512],
                                        start=(c == 0), stop=(c == 7))
                    f.inc("t2a", ins)
                    f.wait(SS, "t2a")
                    f.inc("a2t", SS.activation(s_sb[:], P0[0:64, :], AF.Identity,
                                               bias=bin_sb[:]))
                    f.wait(TT, "a2t")
                    for hc in range(8):
                        pq = (P1, P2)[hc % 2]
                        if hc:
                            f.wait_lag(TT, "a2t", 1)
                        f.inc("t2a", TT.matmul(pq[:], W1k_sb[:, hc * 128:(hc + 1) * 128],
                                               s_sb[:], start=True, stop=True))
                        f.wait(SS, "t2a")
                        f.inc("a2t", SS.activation(hh[:, hc * 512:(hc + 1) * 512], pq[:],
                                                   AF.Relu, bias=b1fm[:, hc:hc + 1]))
                    if b + 1 < NB:
                        ln_ffn(b + 1)
                    f.wait(TT, "a2t")
                    ins = None
                    for hc in range(8):
                        ins = TT.matmul(P3[0:64, :], W2bT_sb[:, hc * 64:(hc + 1) * 64],
                                        hh[:, hc * 512:(hc + 1) * 512],
                                        start=(hc == 0), stop=(hc == 7))
                    f.inc("t2a", ins)
                    f.wait(SS, "t2a")
                    f.inc("a2t", SS.activation(g_sb[:], P3[0:64, :], AF.Identity,
                                               bias=b2_sb[:]))
                    f.wait(TT, "a2t")
                    for t in range(NT):
                        slot = (b * NT + t) * D
                        f.wait(TT, "v2t")
                        ins = None
                        for (s, pp) in ((0, P0), (1, P1)):
                            TT.matmul(pp[:], g_sb[:, t * 128:(t + 1) * 128],
                                      Wout_sb[:, s * 512:(s + 1) * 512],
                                      start=True, stop=False)
                            ins = TT.matmul(pp[:], ones_all[64:65, :],
                                            rows2[64:65, s * 512:(s + 1) * 512],
                                            start=False, stop=True)
                        f.inc("t2v", ins)
                        f.wait(VV, "t2v")
                        VV.scalar_tensor_tensor(R[:, slot:slot + 512], P0[:], 1.0,
                                                R[:, slot:slot + 512], ALU.mult, ALU.add)
                        ins = VV.scalar_tensor_tensor(R[:, slot + 512:slot + D], P1[:],
                                                      1.0, R[:, slot + 512:slot + D],
                                                      ALU.mult, ALU.add)
                        f.inc("v2t", ins)
                bar()

        s4w = contextlib.ExitStack()
        s4w_prefetch = None
        if upto >= 4:
            Wq_sb = s4w.enter_context(nc.sbuf_tensor("Wq_sb", [128, 8 * D], BF16))
            Wk_sb = s4w.enter_context(nc.sbuf_tensor("Wk_sb", [128, 8 * D], BF16))
            Wv_sb = s4w.enter_context(nc.sbuf_tensor("Wv_sb", [128, 8 * D], BF16))

            def s4w_prefetch():
                for c in range(8):
                    dma(nc.gpsimd, Wq_sb[:, c * D:(c + 1) * D],
                        WqT_in[c * 128:(c + 1) * 128, :], q="s4w")
                    dma(nc.gpsimd, Wk_sb[:, c * D:(c + 1) * D],
                        WkT_in[c * 128:(c + 1) * 128, :], q="s4w")
                    dma(nc.gpsimd, Wv_sb[:, c * D:(c + 1) * D],
                        WvT_in[c * 128:(c + 1) * 128, :], q="s4w")

        if upto >= 3:
            emit_ffn("a", prefetch=s4w_prefetch)

        if upto >= 4:
            s4 = contextlib.ExitStack()
            with s4:
                def sb4(name, shape, dtype):
                    return s4.enter_context(nc.sbuf_tensor(name, list(shape), dtype))

                h2b = sb4("h2b", [128, D], BF16)
                h2T = sb4("h2T", [128, 4 * 1024], BF16)
                bvbc = sb4("bvbc", [128, D], F32)
                q_sb = sb4("q_sb", [128, 2 * NB * D], BF16)
                k_sb = sb4("k_sb", [128, 2 * NB * D], BF16)
                v_sb4 = sb4("v_sb4", [128, 2 * NB * D], BF16)
                scrS = sb4("scrS", [128, D], BF16)
                qk_sb = sb4("qk_sb", [128, 24], F32)
                dd = sb4("dd", [128, 16], F32)
                mm4 = sb4("mm4", [128, 4], F32)
                ex4 = sb4("ex4", [128, 16], F32)
                sm4 = sb4("sm4", [128, 8], F32)
                sim4 = sb4("sim4", [128, 16], F32)
                osum = sb4("osum", [128, NB * D], F32)

                dma(nc.gpsimd, bvbc[:], bv_in.ap().broadcast_to([128, D]), q="g")
                bar()

                mixc = {}
                h2T_done = {}

                def s4_stats(pt):
                    ln_group_stats([R[:, (b_ * NT + pt) * D:(b_ * NT + pt + 1) * D]
                                    for b_ in range(NB)], pt * 4, scrS[:])

                def s4_trans(pt):
                    for b in range(NB):
                        slot = (b * NT + pt) * D
                        f.wait(SS, "t2v")
                        f.inc("a2v", ln_apply(pt * 4 + b, R[:, slot:slot + D],
                                              h2b[:]))
                        f.wait(TT, "a2v")
                        f.inc("v2t", transpose8(h2b, h2T, b * 1024, 128, 4 * 1024))
                        f.wait(TT, "v2t")
                    h2T_done[pt] = f.count("v2t")

                def s4_qkv(pt):
                    p = (pt % 2) * NB * D
                    if pt == 0:
                        dwait(TT, "s4w")
                    f.wait_count(TT, "v2t", h2T_done[pt])
                    if pt >= 2:
                        f.wait_count(SS, "vv", mixc[pt - 2])
                    for b in range(NB):
                        f.wait_lag(TT, "a2t", 1)
                        for c in range(8):
                            for (s_, pp, W_) in ((0, P0, Wq_sb), (1, P1, Wq_sb),
                                                 (0, P2, Wk_sb), (1, P3, Wk_sb)):
                                TT.matmul(pp[:], h2T[:, b * 1024 + c * 128:b * 1024 + (c + 1) * 128],
                                          W_[:, c * D + s_ * 512:c * D + (s_ + 1) * 512],
                                          start=(c == 0), stop=False)
                        TT.matmul(P0[:], ones_all[0:1, :], rows2[0:1, 0:512],
                                  start=False, stop=True)
                        TT.matmul(P1[:], ones_all[0:1, :], rows2[0:1, 512:D],
                                  start=False, stop=True)
                        TT.matmul(P2[:], ones_all[32:33, :], rows2[32:33, 0:512],
                                  start=False, stop=True)
                        ins = TT.matmul(P3[:], ones_all[32:33, :], rows2[32:33, 512:D],
                                        start=False, stop=True)
                        f.inc("t2a", ins)
                        f.wait(SS, "t2a")
                        SS.activation(q_sb[:, p + b * D:p + b * D + 512], P0[:],
                                      AF.Identity)
                        SS.activation(q_sb[:, p + b * D + 512:p + (b + 1) * D], P1[:],
                                      AF.Identity)
                        SS.activation(k_sb[:, p + b * D:p + b * D + 512], P2[:],
                                      AF.Identity)
                        f.inc("a2t", SS.activation(k_sb[:, p + b * D + 512:p + (b + 1) * D],
                                                   P3[:], AF.Identity))
                        f.wait(TT, "a2t")
                        ins = None
                        for c in range(8):
                            for (s_, pp) in ((0, P0), (1, P1)):
                                ins = TT.matmul(pp[:], h2T[:, b * 1024 + c * 128:b * 1024 + (c + 1) * 128],
                                                Wv_sb[:, c * D + s_ * 512:c * D + (s_ + 1) * 512],
                                                start=(c == 0), stop=(c == 7))
                        f.inc("t2a", ins)
                        f.wait(SS, "t2a")
                        SS.activation(v_sb4[:, p + b * D:p + b * D + 512], P0[:],
                                      AF.Identity)
                        f.inc("a2t", SS.activation(
                            v_sb4[:, p + b * D + 512:p + (b + 1) * D], P1[:],
                            AF.Identity))
                        f.wait(TT, "a2t")

                def s4_mix(pt):
                    p = (pt % 2) * NB * D
                    g16 = scrA[:].bitcast(BF16)[:, 0:D]
                    if pt == 0:
                        dwait(VV, "g")
                    f.wait(VV, "a2t")
                    for t in range(NB):
                        for j in range(NB):
                            f.inc("vv", VV.scalar_tensor_tensor(
                                g16, q_sb[:, p + t * D:p + (t + 1) * D], 1.0,
                                k_sb[:, p + j * D:p + (j + 1) * D], ALU.mult, ALU.mult,
                                accum_out=qk_sb[:, t * 4 + j:t * 4 + j + 1]))
                    for t in range(NB):
                        f.inc("vv", VV.scalar_tensor_tensor(
                            g16, q_sb[:, p + t * D:p + (t + 1) * D], 1.0,
                            q_sb[:, p + t * D:p + (t + 1) * D], ALU.mult, ALU.mult,
                            accum_out=qk_sb[:, 16 + t:17 + t]))
                        f.inc("vv", VV.scalar_tensor_tensor(
                            g16, k_sb[:, p + t * D:p + (t + 1) * D], 1.0,
                            k_sb[:, p + t * D:p + (t + 1) * D], ALU.mult, ALU.mult,
                            accum_out=qk_sb[:, 20 + t:21 + t]))
                    f.wait(VV, "vv")
                    q2e = bass.AP(qk_sb, 16, [[24, 128], [1, 4], [0, 4]])
                    k2e = bass.AP(qk_sb, 20, [[24, 128], [0, 4], [1, 4]])
                    f.inc("vv", VV.scalar_tensor_tensor(dd[:], qk_sb[:, 0:16], -2.0,
                                                        q2e, ALU.mult, ALU.add))
                    f.wait(VV, "vv")
                    f.inc("vv", VV.tensor_add(dd[:], dd[:], k2e))
                    f.wait(VV, "vv")
                    f.inc("vv", VV.tensor_scalar_max(dd[:], dd[:], 0.0))
                    f.wait(VV, "vv")
                    f.inc("vv", VV.tensor_reduce(
                        mm4[:], dd.ap().rearrange("p (t j) -> p t j", j=4),
                        AX.X, ALU.min))
                    f.wait(VV, "vv")
                    mne = bass.AP(mm4, 0, [[4, 128], [1, 4], [0, 4]])
                    f.inc("v2a", VV.tensor_sub(dd[:], dd[:], mne))
                    f.wait(SS, "v2a")
                    f.inc("a2v", SS.activation(ex4[:], dd[:], AF.Exp, scale=-0.5))
                    f.wait(VV, "a2v")
                    f.inc("vv", VV.tensor_reduce(
                        sm4[:, 0:4], ex4.ap().rearrange("p (t j) -> p t j", j=4),
                        AX.X, ALU.add))
                    f.wait(VV, "vv")
                    f.inc("vv", VV.reciprocal(sm4[:, 4:8], sm4[:, 0:4]))
                    f.wait(VV, "vv")
                    rce = bass.AP(sm4, 4, [[8, 128], [1, 4], [0, 4]])
                    f.inc("vv", VV.tensor_mul(sim4[:], ex4[:], rce))
                    f.wait(VV, "vv")
                    for j in range(NB):
                        for t in range(NB):
                            f.inc("vv", VV.scalar_tensor_tensor(
                                osum[:, t * D:(t + 1) * D],
                                v_sb4[:, p + j * D:p + (j + 1) * D],
                                sim4[:, t * 4 + j:t * 4 + j + 1],
                                bvbc[:] if j == 0 else osum[:, t * D:(t + 1) * D],
                                ALU.mult, ALU.add))
                        f.wait(VV, "vv")
                    for t in range(NB):
                        slot = (t * NT + pt) * D
                        f.inc("vv", VV.tensor_add(R[:, slot:slot + D],
                                                  R[:, slot:slot + D],
                                                  osum[:, t * D:(t + 1) * D]))
                    mixc[pt] = f.count("vv")

                s4_stats(0)
                s4_trans(0)
                s4_qkv(0)
                s4_stats(1)
                s4_trans(1)
                s4_stats(2)
                s4_stats(3)
                s4_mix(0)
                s4_qkv(1)
                s4_trans(2)
                s4_mix(1)
                s4_qkv(2)
                s4_trans(3)
                s4_mix(2)
                s4_qkv(3)
                s4_mix(3)
                bar()
        s4w.close()

        if upto >= 5:
            emit_ffn("b")

        if upto == 11:
            return nc
        bar()
        for b in range(NB):
            for t in range(NT):
                slot = (b * NT + t) * D
                dma(nc.sync, out[b * NPOS + t * 128:b * NPOS + (t + 1) * 128, :],
                    R[:, slot:slot + D], q="st")
        dwait(nc.sync, "st")
        bar()
    return nc


def _prep_weights(inputs):
    g = {k: np.asarray(v, np.float32) for k, v in inputs.items()}
    fm = lambda v, k: np.ascontiguousarray(v.reshape(k, 128).T)
    row = lambda v: np.ascontiguousarray(v.reshape(1, -1))
    W = {}
    W["qkv_wT"] = np.ascontiguousarray(g["L0_attn_a_qkv_w"].T)
    W["ln1_w_fm"] = fm(g["L0_attn_a_ln1_w"], 8)
    W["ln1_b_fm"] = fm(g["L0_attn_a_ln1_b"], 8)
    W["qkv_b_row"] = row(g["L0_attn_a_qkv_b"])
    W["proj_wT"] = np.ascontiguousarray(g["L0_attn_a_proj_w"].T)
    W["proj_b_row"] = row(g["L0_attn_a_proj_b"])
    W["ln2_w_fm"] = fm(g["L0_attn_a_ln2_w"], 8)
    W["ln2_b_fm"] = fm(g["L0_attn_a_ln2_b"], 8)
    W["fc1_wT"] = np.ascontiguousarray(g["L0_attn_a_fc1_w"].T)
    W["fc1_b_row"] = row(g["L0_attn_a_fc1_b"])
    W["fc2_wT"] = np.ascontiguousarray(g["L0_attn_a_fc2_w"].T)
    W["fc2_b_row"] = row(g["L0_attn_a_fc2_b"])
    W["WqT"] = np.ascontiguousarray(g["L0_attn_b_Wq"].T)
    W["WkT"] = np.ascontiguousarray(g["L0_attn_b_Wk"].T)
    W["WvT"] = np.ascontiguousarray(g["L0_attn_b_Wv"].T)
    W["bq_row"] = row(g["L0_attn_b_bq"])
    W["bk_row"] = row(g["L0_attn_b_bk"])
    W["bv_row"] = row(g["L0_attn_b_bv"])
    W["Win"] = np.ascontiguousarray(g["L0_ffn_Win"])
    W["bin_fm"] = np.ascontiguousarray(g["L0_ffn_bin_"].reshape(64, 1))
    W1 = g["L0_ffn_W1"]
    W1k = np.zeros((64, D), np.float32)
    W1k[np.arange(64)[:, None],
        np.arange(64)[:, None] * 16 + np.arange(16)[None, :]] = W1
    W["W1kron"] = W1k
    W["b1_fm"] = fm(g["L0_ffn_b1"].reshape(-1), 8)
    W2 = g["L0_ffn_W2"]
    W2b = np.zeros((D, 64), np.float32)
    W2b[np.arange(64)[:, None] * 16 + np.arange(16)[None, :],
        np.arange(64)[:, None]] = W2
    W["W2blkT"] = W2b
    W["b2_fm"] = np.ascontiguousarray(g["L0_ffn_b2"].reshape(64, 1))
    W["Wout"] = np.ascontiguousarray(g["L0_ffn_Wout"])
    W["bout_row"] = row(g["L0_ffn_bout"])
    return W


def _host_inputs(inputs, core, weights):
    x = np.asarray(inputs["L0_attn_a_x"], np.float32)
    c0 = core * NPOS
    m = {"x_own": np.ascontiguousarray(x[:, c0:c0 + NPOS, :].reshape(NB * NPOS, D))}
    if core == 0:
        m["x_hl"] = np.zeros((NB * 128, D), np.float32)
    else:
        m["x_hl"] = np.ascontiguousarray(x[:, c0 - 128:c0, :].reshape(NB * 128, D))
    if core == NCORES - 1:
        m["x_hr"] = np.zeros((NB * 128, D), np.float32)
    else:
        m["x_hr"] = np.ascontiguousarray(
            x[:, c0 + NPOS:c0 + NPOS + 128, :].reshape(NB * 128, D))
    maskv = np.ones((128, 6), np.float32)
    if core == 0:
        maskv[:, 0] = 0.0
    if core == NCORES - 1:
        maskv[:, 5] = 0.0
    m["maskv"] = maskv
    m.update(weights)
    return m


def kernel(_upto=5, **inputs):
    weights = _prep_weights(inputs)
    nc = build_program(upto=_upto)
    in_maps = [_host_inputs(inputs, c, weights) for c in range(NCORES)]
    res = run_bass_kernel_spmd(nc, in_maps, core_ids=list(range(NCORES)))
    outs = [res.results[c]["out"].reshape(NB, NPOS, D) for c in range(NCORES)]
    return np.ascontiguousarray(np.concatenate(outs, axis=1))


# version 25
# speedup vs baseline: 1.1776x; 1.1776x over previous
import contextlib
import numpy as np
import ml_dtypes
import concourse.bass as bass
import concourse.mybir as mybir
from concourse.bass_utils import run_bass_kernel_spmd

F32 = mybir.dt.float32
BF16 = mybir.dt.bfloat16
AF = mybir.ActivationFunctionType
ALU = mybir.AluOpType
AX = mybir.AxisListType

NCORES = 8
NB = 4
NPOS = 512
NT = 4
D = 1024
DFF = 4096
NH = 16
HD = 64
EPS = 1e-5
VW = 65


class Flow:
    def __init__(self, nc, es):
        self.nc, self.es, self.sems, self.counts = nc, es, {}, {}

    def _get(self, name):
        if name not in self.sems:
            self.sems[name] = self.es.enter_context(self.nc.semaphore(name))
            self.counts[name] = 0
        return self.sems[name]

    def inc(self, name, instr, n=1):
        instr.then_inc(self._get(name), n)
        self.counts[name] += n
        return self.counts[name]

    def wait(self, eng, name):
        eng.wait_ge(self._get(name), self.counts[name])

    def wait_count(self, eng, name, count):
        eng.wait_ge(self._get(name), count)

    def count(self, name):
        self._get(name)
        return self.counts[name]

    def wait_lag(self, eng, name, lag):
        eng.wait_ge(self._get(name), max(0, self.counts[name] - lag))


def build_program(upto=5):
    nc = bass.Bass("TRN2", target_bir_lowering=False, detect_race_conditions=False)

    def din(name, shape, dtype=F32):
        return nc.dram_tensor(name, list(shape), dtype, kind="ExternalInput")

    x_own = din("x_own", [NB * NPOS, D])
    x_hl = din("x_hl", [NB * 128, D])
    x_hr = din("x_hr", [NB * 128, D])
    maskv_in = din("maskv", [128, 6])
    qkv_wT_in = din("qkv_wT", [D, 3 * D])
    ln1_w_in = din("ln1_w_fm", [128, 8])
    ln1_b_in = din("ln1_b_fm", [128, 8])
    qkv_b_in = din("qkv_b_row", [1, 3 * D])
    proj_wT_in = din("proj_wT", [D, D])
    proj_b_in = din("proj_b_row", [1, D])
    ln2_w_in = din("ln2_w_fm", [128, 8])
    ln2_b_in = din("ln2_b_fm", [128, 8])
    fc1_wT_in = din("fc1_wT", [D, DFF])
    fc1_b_in = din("fc1_b_row", [1, DFF])
    fc2_wT_in = din("fc2_wT", [DFF, D])
    fc2_b_in = din("fc2_b_row", [1, D])
    WqT_in = din("WqT", [D, D])
    WkT_in = din("WkT", [D, D])
    WvT_in = din("WvT", [D, D])
    bq_in = din("bq_row", [1, D])
    bk_in = din("bk_row", [1, D])
    bv_in = din("bv_row", [1, D])
    Win_in = din("Win", [D, 64])
    bin_in = din("bin_fm", [64, 1])
    W1k_in = din("W1kron", [64, D])
    b1_in = din("b1_fm", [128, 8])
    W2bT_in = din("W2blkT", [D, 64])
    b2_in = din("b2_fm", [64, 1])
    Wout_in = din("Wout", [64, D])
    bout_in = din("bout_row", [1, D])

    out = nc.dram_tensor("out", [NB * NPOS, D], F32, kind="ExternalOutput")
    fc1bf = nc.dram_tensor("fc1bf", [D, DFF], BF16, kind="Internal")

    es = contextlib.ExitStack()
    f = Flow(nc, es)
    bar = nc.all_engine_barrier

    def dma(eng, dst, src, q="ld", **kw):
        ins = eng.dma_start(dst, src, **kw)
        f.inc("dma_" + q, ins, 16)
        return ins

    def dwait(eng, q="ld"):
        f.wait(eng, "dma_" + q)

    with es:
        def sb(name, shape, dtype):
            return es.enter_context(nc.sbuf_tensor(name, list(shape), dtype))

        def psum(name, shape, dtype=F32):
            return es.enter_context(nc.psum_tensor(name, list(shape), dtype))

        R = sb("R", [128, NB * NT * D], F32)
        ident = sb("ident", [128, 128], BF16)
        identF = sb("identF", [128, 128], F32)
        ones_all = sb("ones_all", [128, 128], BF16)
        scrA = sb("scrA", [128, D], F32)
        st = sb("st", [128, 64], F32)
        maskv = sb("maskv_sb", [128, 6], F32)
        rows1 = sb("rows1", [128, D], BF16)
        rows2 = sb("rows2", [128, D], BF16)
        bqk_fm = sb("bqk_fm", [128, 16], F32)
        bf1_fm = sb("bf1_fm", [128, 32], F32)
        lnw = sb("lnw", [128, 32], F32)
        lnbf = sb("lnbf", [128, 16], BF16)
        b1fm = sb("b1fm", [128, 8], F32)
        bin_sb = sb("bin_sb", [64, 1], F32)
        b2_sb = sb("b2_sb", [64, 1], F32)

        P0 = psum("P0", [128, 512])
        P1 = psum("P1", [128, 512])
        P2 = psum("P2", [128, 512])
        P3 = psum("P3", [128, 512])
        P4 = psum("P4", [128, 512])
        Psim = psum("Psim", [128, 384])
        Psim2 = psum("Psim2", [128, 512])
        Ptr = psum("Ptr", [128, 1024], BF16)

        TT, VV, SS, GG = nc.tensor, nc.scalar and nc.vector, nc.scalar, nc.gpsimd
        TT, VV, SS, GG = nc.tensor, nc.vector, nc.scalar, nc.gpsimd
        negmax = st[:, 32:48]
        rcp = st[:, 48:64]

        def ln_stats_apply(x_ap, out_ap, gout):
            ln_group_stats([x_ap], 0, gout)
            return ln_apply(0, x_ap, out_ap)

        def ln_group_stats(x_aps, base, gout):
            n = len(x_aps)
            for i, xap in enumerate(x_aps):
                f.inc("a2v", SS.activation(gout, xap, AF.Square,
                                           accum_out=st[:, 4 + i:5 + i]))
                f.inc("vv", VV.tensor_scalar(scrA[:], xap, 1.0, 0.0, ALU.mult, ALU.add,
                                             accum_out=st[:, 24 + base + i:25 + base + i]))
            f.wait(VV, "vv")
            f.wait(VV, "a2v")
            f.inc("vv", VV.tensor_scalar(st[:, 24 + base:24 + base + n],
                                         st[:, 24 + base:24 + base + n],
                                         1.0 / D, None, ALU.mult))
            f.inc("vv", VV.tensor_scalar(st[:, 4:4 + n], st[:, 4:4 + n], 1.0 / D, EPS,
                                         ALU.mult, ALU.add))
            f.wait(VV, "vv")
            f.inc("vv", VV.scalar_tensor_tensor(st[:, 0:n], st[:, 24 + base:24 + base + n],
                                                1.0, st[:, 24 + base:24 + base + n],
                                                ALU.mult, ALU.mult))
            f.wait(VV, "vv")
            f.inc("vv", VV.tensor_sub(st[:, 4:4 + n], st[:, 4:4 + n], st[:, 0:n]))
            f.wait(VV, "vv")
            f.inc("v2a", VV.reciprocal(st[:, 0:n], st[:, 4:4 + n]))
            f.wait(SS, "v2a")
            f.inc("a2v", SS.activation(st[:, 8 + base:8 + base + n], st[:, 0:n], AF.Sqrt))
            f.wait(VV, "a2v")
            f.inc("v2a", VV.scalar_tensor_tensor(st[:, 24 + base:24 + base + n],
                                                 st[:, 24 + base:24 + base + n], -1.0,
                                                 st[:, 8 + base:8 + base + n],
                                                 ALU.mult, ALU.mult))
            f.wait(SS, "v2a")

        def ln_apply(k, x_ap, out_ap):
            return SS.activation(out_ap, x_ap, AF.Identity,
                                 bias=st[:, 24 + k:25 + k], scale=st[:, 8 + k:9 + k])

        def transpose8(src_sb, dst_sb, dst_off, dst_stride, row_len):
            ins = None
            for c in range(8):
                ins = TT.transpose(Ptr[:, c * 128:(c + 1) * 128],
                                   src_sb[:, c * 128:(c + 1) * 128], ident[:])
            f.inc("t2v", ins)
            f.wait(VV, "t2v")
            dst = bass.AP(dst_sb, dst_off, [[row_len, 128], [dst_stride, 8], [1, 128]])
            return VV.tensor_copy(dst, Ptr[:])

        def row_to_fm(src_row_ap_fn, n_chunks, pcol0, evac_dst, base_p=0):
            ins = None
            for i in range(n_chunks):
                ins = TT.transpose(P4[:, pcol0 + i:pcol0 + i + 1], src_row_ap_fn(i),
                                   identF[base_p:base_p + 1, base_p:base_p + 1])
            f.inc("t2v", ins)
            f.wait(VV, "t2v")
            return VV.tensor_copy(evac_dst, P4[:, pcol0:pcol0 + n_chunks])

        GG.memset(ident[:], 0.0)
        GG.memset(identF[:], 0.0)
        GG.memset(ones_all[:], 1.0)
        bar()
        GG.affine_select(out=ident[:], in_=ident[:], compare_op=ALU.not_equal,
                         fill=1.0, base=0, pattern=[[-1, 128]], channel_multiplier=1)
        GG.affine_select(out=identF[:], in_=identF[:], compare_op=ALU.not_equal,
                         fill=1.0, base=0, pattern=[[-1, 128]], channel_multiplier=1)
        bar()

        dma(nc.sync, maskv[:], maskv_in[:])
        dma(nc.sync, lnw[:, 0:8], ln1_w_in[:])
        dma(nc.sync, lnw[:, 8:16], ln1_b_in[:])
        dma(nc.sync, lnw[:, 16:24], ln2_w_in[:])
        dma(nc.sync, lnw[:, 24:32], ln2_b_in[:])
        dma(nc.sync, b1fm[:], b1_in[:])
        dma(nc.sync, bin_sb[:], bin_in[:])
        dma(nc.sync, b2_sb[:], b2_in[:])
        dma(nc.gpsimd, rows1[64:65, :], fc2_b_in[:], q="g")
        dma(nc.gpsimd, rows2[0:1, :], bq_in[:], q="g")
        dma(nc.gpsimd, rows2[32:33, :], bk_in[:], q="g")
        dma(nc.gpsimd, rows2[64:65, :], bout_in[:], q="g")

        dma(nc.sync, R[0:1, 0:3 * D], qkv_b_in[:])
        dma(nc.sync, R[32:33, 0:D], proj_b_in[:])
        dma(nc.sync, R[32:33, 4096:4096 + DFF], fc1_b_in[:])
        dwait(TT, "g")
        dwait(VV, "g")
        dwait(SS, "g")
        dwait(TT)
        dwait(VV)
        dwait(SS)
        bar()
        VV.tensor_copy(lnbf[:, 0:8], lnw[:, 8:16])
        VV.tensor_copy(lnbf[:, 8:16], lnw[:, 24:32])
        bar()
        with nc.sbuf_tensor("fc1all", [128, 8 * DFF], BF16) as fc1all:
            for c in range(8):
                dma(nc.gpsimd, fc1all[:, c * DFF:(c + 1) * DFF],
                    fc1_wT_in[c * 128:(c + 1) * 128, :], q="g")
            dwait(TT, "g")
            bar()
            for s in range(8):
                for c in range(8):
                    TT.matmul(P0[0:1, :], lnbf[:, 8 + c:9 + c],
                              fc1all[:, c * DFF + s * 512:c * DFF + (s + 1) * 512],
                              start=(c == 0), stop=(c == 7))
                bar()
                VV.scalar_tensor_tensor(
                    R[64:65, 4096 + s * 512:4096 + (s + 1) * 512], P0[0:1, :], 1.0,
                    R[32:33, 4096 + s * 512:4096 + (s + 1) * 512], ALU.mult, ALU.add)
                bar()
            row_to_fm(lambda i: R[64:65, 4096 + i * 128:4096 + (i + 1) * 128], 32, 0,
                      bf1_fm[:], base_p=64)
            bar()
            for c in range(8):
                VV.tensor_scalar(fc1all[:, c * DFF:(c + 1) * DFF],
                                 fc1all[:, c * DFF:(c + 1) * DFF],
                                 lnw[:, 16 + c:17 + c], None, ALU.mult)
            bar()
            for c in range(8):
                dma(nc.sync, fc1bf[c * 128:(c + 1) * 128, :],
                    fc1all[:, c * DFF:(c + 1) * DFF], q="pst")
            dwait(nc.sync, "pst")
            bar()


        att_es = contextlib.ExitStack()
        qkvW = att_es.enter_context(nc.sbuf_tensor("qkvW", [128, 8 * 3072], BF16))
        projW = att_es.enter_context(nc.sbuf_tensor("projW", [128, 8 * D], BF16))
        for c in range(8):
            dma(nc.gpsimd, qkvW[:, c * 3072:(c + 1) * 3072],
                qkv_wT_in[c * 128:(c + 1) * 128, :], q="g")
            dma(nc.gpsimd, projW[:, c * D:(c + 1) * D],
                proj_wT_in[c * 128:(c + 1) * 128, :], q="g")
        dwait(TT, "g")
        bar()


        for s in range(6):
            for c in range(8):
                TT.matmul(P0[0:1, :], lnbf[:, c:c + 1],
                          qkvW[:, c * 3072 + s * 512:c * 3072 + (s + 1) * 512],
                          start=(c == 0), stop=(c == 7))
            bar()
            VV.scalar_tensor_tensor(R[0:1, 4096 + s * 512:4096 + (s + 1) * 512],
                                    P0[0:1, :], 1.0, R[0:1, s * 512:(s + 1) * 512],
                                    ALU.mult, ALU.add)
            bar()
        VV.tensor_scalar(R[0:1, 4096:4096 + D], R[0:1, 4096:4096 + D], 0.125, None,
                         ALU.mult)
        bar()
        for c in range(8):
            VV.tensor_scalar(qkvW[:, c * 3072:c * 3072 + D],
                             qkvW[:, c * 3072:c * 3072 + D],
                             lnw[:, c:c + 1], 0.125, ALU.mult, ALU.mult)
            VV.tensor_scalar(qkvW[:, c * 3072 + D:(c + 1) * 3072],
                             qkvW[:, c * 3072 + D:(c + 1) * 3072],
                             lnw[:, c:c + 1], None, ALU.mult)
        bar()
        row_to_fm(lambda i: R[0:1, 4096 + i * 128:4096 + (i + 1) * 128], 16, 0,
                  bqk_fm[:])
        VV.tensor_copy(rows1[0:1, :], R[0:1, 4096 + 2 * D:4096 + 3 * D])
        bar()
        VV.tensor_copy(rows1[32:33, :], R[32:33, 0:D])
        bar()

        if upto == 0:
            dbg1 = nc.dram_tensor("dbg1", [128, 64], F32, kind="ExternalOutput")
            dbg2 = nc.dram_tensor("dbg2", [128, 3072], BF16, kind="ExternalOutput")
            dbg3 = nc.dram_tensor("dbg3", [128, 2 * D], BF16, kind="ExternalOutput")
            bar()
            dma(nc.sync, dbg1[:, 0:16], bqk_fm[:], q="st")
            dma(nc.sync, dbg1[:, 16:48], bf1_fm[:], q="st")
            dma(nc.sync, dbg1[:, 48:56], lnw[:, 0:8], q="st")
            dma(nc.sync, dbg2[:], qkvW[:, 0:3072], q="st")
            dma(nc.sync, dbg3[:, 0:D], rows1[:], q="st")
            dma(nc.sync, dbg3[:, D:2 * D], rows2[:], q="st")
        if upto >= 1 or upto == 11:
            s1 = contextlib.ExitStack()
            with s1:
                def sb1(name, shape, dtype):
                    return s1.enter_context(nc.sbuf_tensor(name, list(shape), dtype))

                xcur = sb1("xcur", [128, 2 * D], F32)
                xh0 = sb1("xh0", [128, 2 * D], F32)
                a1 = sb1("a1", [128, D], BF16)
                a1T = sb1("a1T", [128, 8 * 768], BF16)
                qfm = sb1("qfm", [128, 8 * 512], BF16)
                kfm = sb1("kfm", [128, 8 * 768], BF16)
                v_sb = sb1("v_sb", [128, 6 * NH * VW], BF16)
                o_tok = sb1("o_tok", [128, D], BF16)
                oT = sb1("oT", [128, D], BF16)
                p_sb = sb1("p_sb", [128, 2 * 384], BF16)
                pt_sb = sb1("pt_sb", [128, 2 * 384], BF16)
                VROW = 6 * NH * VW
                xfree_c = [0, 0]

                for j in range(6):
                    dst = bass.AP(v_sb, j * NH * VW + HD, [[VROW, 128], [VW, NH]])
                    src = bass.AP(maskv, j, [[6, 128], [0, NH]])
                    VV.tensor_copy(dst, src)
                bar()

                ldx = {}
                a1T_done = {}
                last_ptr_evac = [("ptc", f.count("ptc"))]
                f.count("ptcS")
                p01_free = [("v2t", f.count("v2t"))]
                ln_p01 = [("v2t", f.count("v2t"))]

                def xsrc(b, j):
                    if j == 0:
                        return x_hl[b * 128:(b + 1) * 128, :]
                    if j == 5:
                        return x_hr[b * 128:(b + 1) * 128, :]
                    return x_own[b * NPOS + (j - 1) * 128:b * NPOS + j * 128, :]

                def ln_tile(b, j):
                    g = b * 6 + j
                    f.wait(VV, "a2v")
                    nb_, nj = (b, j + 1) if j + 1 < 6 else (b + 1, 0)
                    if nb_ < NB:
                        f.wait_count(nc.sync, "a2v", xfree_c[(g + 1) % 2])
                        dma(nc.sync, xcur[:, ((g + 1) % 2) * D:((g + 1) % 2 + 1) * D],
                            xsrc(nb_, nj))
                        ldx[g + 1] = f.count("dma_ld")
                    xc = xcur[:, (g % 2) * D:(g % 2 + 1) * D]
                    xd = xh0[:, (g % 2) * D:(g % 2 + 1) * D]
                    gact = xh0[:, ((g + 1) % 2) * D:((g + 1) % 2 + 1) * D]
                    f.wait_count(VV, "dma_ld", ldx[g])
                    f.wait_count(SS, "dma_ld", ldx[g])
                    ln_group_stats([xc], 0, gact)
                    f.inc("a2v", ln_apply(0, xc, xd))
                    xfree_c[g % 2] = f.count("a2v")
                    f.wait(VV, "a2v")
                    ln_group_stats([xd], 0, gact)
                    f.wait(SS, "t2v")
                    f.inc("a2v", ln_apply(0, xd, a1[:]))
                    f.wait(TT, "a2v")
                    f.wait_count(TT, p01_free[0][0], p01_free[0][1])
                    P0b = P0[:].bitcast(BF16)
                    ins = None
                    for c in range(8):
                        ins = TT.transpose(P0b[:, c * 128:(c + 1) * 128],
                                           a1[:, c * 128:(c + 1) * 128], ident[:])
                    f.inc("t2v", ins)
                    f.wait(VV, "t2v")
                    dst = bass.AP(a1T, j * 128, [[8 * 768, 128], [768, 8], [1, 128]])
                    cnt = f.inc("v2t", VV.tensor_copy(dst, P0b[:]))
                    ln_p01[0] = ("v2t", cnt)
                    if j == 5:
                        a1T_done[b] = cnt

                def qkv_phase(b):
                    f.wait_count(TT, "v2t", a1T_done[b])
                    f.wait_count(TT, ln_p01[0][0], ln_p01[0][1])
                    f.wait(TT, "a2t")
                    for m in range(8):
                        pq = (P0, P1)[m % 2]
                        if m:
                            f.wait_lag(TT, "a2t", 1)
                        ins = None
                        for c in range(8):
                            ins = TT.matmul(pq[:],
                                            qkvW[:, c * 3072 + m * 128:c * 3072 + (m + 1) * 128],
                                            a1T[:, c * 768 + 128:c * 768 + 640],
                                            start=(c == 0), stop=(c == 7))
                        f.inc("t2a", ins)
                        f.wait(SS, "t2a")
                        f.inc("a2t", SS.activation(qfm[:, m * 512:(m + 1) * 512], pq[:],
                                                   AF.Identity, bias=bqk_fm[:, m:m + 1]))
                    f.wait(TT, "a2t")
                    for m in range(8):
                        ka, kb = ((P0, P1), (P2, P3))[m % 2]
                        if m:
                            f.wait_lag(TT, "a2t", 1)
                        ins = None
                        for (off, w_, pk) in ((0, 512, ka), (512, 256, kb)):
                            for c in range(8):
                                ins = TT.matmul(
                                    pk[:, 0:w_],
                                    qkvW[:, c * 3072 + D + m * 128:c * 3072 + D + (m + 1) * 128],
                                    a1T[:, c * 768 + off:c * 768 + off + w_],
                                    start=(c == 0), stop=(c == 7))
                        f.inc("t2a", ins)
                        f.wait(SS, "t2a")
                        SS.activation(kfm[:, m * 768:m * 768 + 512], ka[:],
                                      AF.Identity, bias=bqk_fm[:, 8 + m:9 + m])
                        f.inc("a2t", SS.activation(kfm[:, m * 768 + 512:(m + 1) * 768],
                                                   kb[:, 0:256], AF.Identity,
                                                   bias=bqk_fm[:, 8 + m:9 + m]))
                    f.wait(TT, "a2t")
                    for j in range(6):
                        va, vb = ((P2, P3), (P0, P1))[j % 2]
                        if j:
                            f.wait_lag(TT, "v2t", 1)
                        ins = None
                        for (s, pv) in ((0, va), (1, vb)):
                            for c in range(8):
                                TT.matmul(
                                    pv[:],
                                    a1T[:, c * 768 + j * 128:c * 768 + (j + 1) * 128],
                                    qkvW[:, c * 3072 + 2 * D + s * 512:c * 3072 + 2 * D + (s + 1) * 512],
                                    start=(c == 0), stop=False)
                            ins = TT.matmul(pv[:], ones_all[0:1, :],
                                            rows1[0:1, s * 512:(s + 1) * 512],
                                            start=False, stop=True)
                        f.inc("t2v", ins)
                        f.wait(VV, "t2v")
                        base_ = j * NH * VW
                        d0 = bass.AP(v_sb, base_, [[VROW, 128], [VW, 8], [1, HD]])
                        d1 = bass.AP(v_sb, base_ + 8 * VW, [[VROW, 128], [VW, 8], [1, HD]])
                        VV.tensor_scalar(d0, va[:], maskv[:, j:j + 1], None, ALU.mult)
                        ins = VV.tensor_scalar(d1, vb[:], maskv[:, j:j + 1], None,
                                               ALU.mult)
                        f.inc("v2t", ins)
                    f.wait(TT, "v2t")

                def attention_batch(b):
                    f.wait(TT, "a2t")
                    heads = [(w, h) for w in range(NT) for h in range(NH)]
                    nhd = len(heads)
                    simc, Tc, copyc, expc, maxc, pvc = {}, {}, {}, {}, {}, {}
                    normc, projevc = {}, {}

                    def po_slot(h):
                        return ((P2, h * VW) if h < 7 else
                                (P3, (h - 7) * VW) if h < 14 else
                                (P4, (h - 14) * VW))

                    def emit_sim(i):
                        w, h = heads[i]
                        S = (Psim[:], Psim2[:, 0:384])[i % 2]
                        if i >= 2:
                            f.wait_count(TT, "a2t", expc[i - 2])
                        hp = (h % 2) * 64
                        hc = h // 2
                        simc[i] = f.inc("t2v", TT.matmul(
                            S,
                            qfm[hp:hp + 64, hc * 512 + w * 128:hc * 512 + (w + 1) * 128],
                            kfm[hp:hp + 64, hc * 768 + w * 128:hc * 768 + w * 128 + 384],
                            start=True, stop=True))
                        f.wait_count(VV, "t2v", simc[i])
                        maxc[i] = f.inc("v2a", VV.reduce_max(
                            negmax[:, h:h + 1], S, AX.X, negate=True))
                        f.wait_count(SS, "v2a", maxc[i])
                        if i >= 2:
                            f.wait_count(SS, "t2v", Tc[i - 2])
                        expc[i] = f.inc("a2t", SS.activation(
                            p_sb[:, (i % 2) * 384:(i % 2) * 384 + 384], S, AF.Exp,
                            bias=negmax[:, h:h + 1]))

                    def emit_T(i):
                        f.wait_count(TT, "a2t", expc[i])
                        f.wait_count(TT, last_ptr_evac[0][0], last_ptr_evac[0][1])
                        ins = None
                        for kc in range(3):
                            ins = TT.transpose(
                                Ptr[:, kc * 128:(kc + 1) * 128],
                                p_sb[:, (i % 2) * 384 + kc * 128:(i % 2) * 384 + (kc + 1) * 128],
                                ident[:])
                        Tc[i] = f.inc("t2v", ins)
                        dst = pt_sb[:, (i % 2) * 384:(i % 2) * 384 + 384]
                        if i % 2 == 0:
                            f.wait_count(VV, "t2v", Tc[i])
                            if i >= 2:
                                f.wait_count(VV, "t2v", pvc[i - 2])
                            copyc[i] = ("ptc", f.inc("ptc", VV.tensor_copy(dst, Ptr[:, 0:384])))
                        else:
                            f.wait_count(SS, "t2v", Tc[i])
                            if i >= 2:
                                f.wait_count(SS, "t2v", pvc[i - 2])
                            copyc[i] = ("ptcS", f.inc("ptcS", SS.activation(
                                dst, Ptr[:, 0:384], AF.Identity)))
                        last_ptr_evac[0] = copyc[i]

                    def emit_PV(i):
                        w, h = heads[i]
                        f.wait_count(TT, copyc[i][0], copyc[i][1])
                        if h == 0 and w >= 1:
                            f.wait_count(TT, "v2t", normc[w - 1])
                        po, oc = po_slot(h)
                        ins = None
                        for kc in range(3):
                            ins = TT.matmul(
                                po[:, oc:oc + VW],
                                pt_sb[:, (i % 2) * 384 + kc * 128:(i % 2) * 384 + (kc + 1) * 128],
                                v_sb[:, (w + kc) * NH * VW + h * VW:(w + kc) * NH * VW + (h + 1) * VW],
                                start=(kc == 0), stop=(kc == 2))
                        pvc[i] = f.inc("t2v", ins)

                    def emit_norm_part(w, part):
                        lo, hi, bank = ((0, 7, P2), (7, 14, P3), (14, 16, P4))[part]
                        f.wait_count(VV, "t2v", pvc[w * NH + hi - 1])
                        f.inc("vv", VV.reciprocal(
                            rcp[:, lo:hi], bass.AP(bank, HD, [[512, 128], [VW, hi - lo]])))
                        f.wait(VV, "vv")
                        ins = None
                        for h in range(lo, hi):
                            po, oc = po_slot(h)
                            ins = VV.tensor_scalar(o_tok[:, h * HD:(h + 1) * HD],
                                                   po[:, oc:oc + HD], rcp[:, h:h + 1],
                                                   None, ALU.mult)
                        cnt = f.inc("v2t", ins)
                        ch = ((0, 3), (3, 7), (7, 8))[part]
                        f.wait_count(TT, "v2t", cnt)
                        f.wait_count(TT, last_ptr_evac[0][0], last_ptr_evac[0][1])
                        ins = None
                        for c in range(ch[0], ch[1]):
                            ins = TT.transpose(Ptr[:, c * 128:(c + 1) * 128],
                                               o_tok[:, c * 128:(c + 1) * 128], ident[:])
                        f.inc("t2v", ins)
                        f.wait(VV, "t2v")
                        dst = bass.AP(oT, ch[0] * 128,
                                      [[D, 128], [128, ch[1] - ch[0]], [1, 128]])
                        cnt2 = f.inc("v2t", VV.tensor_copy(
                            dst, Ptr[:, ch[0] * 128:ch[1] * 128]))
                        last_ptr_evac[0] = ("v2t", cnt2)
                        if part == 2:
                            normc[w] = cnt2

                    def emit_proj(w):
                        f.wait_count(TT, "v2t", normc[w])
                        if w >= 1:
                            f.wait_count(TT, "v2t", projevc[w - 1])
                        f.wait_count(TT, ln_p01[0][0], ln_p01[0][1])
                        ins = None
                        for (s, pp) in ((0, P0), (1, P1)):
                            for c in range(8):
                                TT.matmul(pp[:], oT[:, c * 128:(c + 1) * 128],
                                          projW[:, c * D + s * 512:c * D + (s + 1) * 512],
                                          start=(c == 0), stop=False)
                            ins = TT.matmul(pp[:], ones_all[32:33, :],
                                            rows1[32:33, s * 512:(s + 1) * 512],
                                            start=False, stop=True)
                        prj = f.inc("t2a", ins)
                        f.wait_count(SS, "t2a", prj)
                        slot = (b * NT + w) * D
                        SS.activation(R[:, slot:slot + 512], P0[:], AF.Identity)
                        ins = SS.activation(R[:, slot + 512:slot + D], P1[:], AF.Identity)
                        projevc[w] = f.inc("v2t", ins)
                        p01_free[0] = ("v2t", projevc[w])

                    pending_proj = []
                    ln_tasks = [(b + 1, j) for j in range(6)] if b + 1 < NB else []
                    ln_ready = []
                    for i in range(nhd + 2):
                        if i < nhd:
                            emit_sim(i)
                        if 1 <= i <= nhd:
                            emit_T(i - 1)
                        if 2 <= i <= nhd + 1:
                            emit_PV(i - 2)
                            w2, h2_ = heads[i - 2]
                            if h2_ == 6:
                                emit_norm_part(w2, 0)
                            elif h2_ == 13:
                                emit_norm_part(w2, 1)
                            elif h2_ == NH - 1:
                                emit_norm_part(w2, 2)
                                pending_proj.append(w2)
                        if pending_proj and (i >= nhd + 1 or
                                             (i - 2 >= 0 and i - 2 - pending_proj[0] * NH >= 21)):
                            wp = pending_proj.pop(0)
                            emit_proj(wp)
                            if ln_tasks:
                                ln_ready.append(ln_tasks.pop(0))
                        if ln_ready and i < nhd and (i % 8) == 4:
                            bb, jj = ln_ready.pop(0)
                            ln_tile(bb, jj)
                    while pending_proj:
                        emit_proj(pending_proj.pop(0))
                    for bb, jj in ln_ready + ln_tasks:
                        ln_tile(bb, jj)
                    f.wait(TT, "v2t")
                    f.wait(TT, "ptc")
                    f.wait(TT, "ptcS")

                dma(nc.sync, xcur[:, 0:D], xsrc(0, 0))
                ldx[0] = f.count("dma_ld")
                for j in range(6):
                    ln_tile(0, j)
                for b in range(NB):
                    qkv_phase(b)
                    attention_batch(b)
                bar()

        att_es.close()

        if upto >= 2:
            s2 = contextlib.ExitStack()
            with s2:
                def sb2(name, shape, dtype):
                    return s2.enter_context(nc.sbuf_tensor(name, list(shape), dtype))

                fc2W = sb2("fc2W", [128, 32 * D], BF16)
                fc1s = sb2("fc1s", [128, 2 * D], BF16)
                x2T = sb2("x2T", [128, 8 * 512], BF16)
                yb = sb2("yb", [128, 32 * 512], BF16)
                xall = sb2("xall", [128, 2 * D], F32)
                xs = sb2("xs", [128, 2 * D], F32)
                h0b = sb2("h0b", [128, D], F32)
                h2 = sb2("h2", [128, D], BF16)

                bar()

                xsf = [0, 0]
                h0f = [0, 0]

                def h0_phase(b):
                    for t in range(NT):
                        g = b * NT + t
                        f.wait_count(nc.sync, "a2v", h0f[g % 2])
                        dma(nc.sync, xall[:, (g % 2) * D:(g % 2 + 1) * D],
                            x_own[b * NPOS + t * 128:b * NPOS + (t + 1) * 128, :])
                        ldg = f.count("dma_ld")
                        f.wait_count(VV, "dma_ld", ldg)
                        f.wait_count(SS, "dma_ld", ldg)
                        xc = xall[:, (g % 2) * D:(g % 2 + 1) * D]
                        f.wait(VV, "a2v")
                        ln_group_stats([xc], 4, h2[:])
                        if t or b:
                            f.wait(SS, "vv")
                        f.inc("a2v", ln_apply(4, xc, h0b[:]))
                        h0f[g % 2] = f.count("a2v")
                        f.wait(VV, "a2v")
                        slot = g * D
                        f.inc("vv", VV.tensor_add(R[:, slot:slot + D],
                                                  R[:, slot:slot + D], h0b[:]))

                for b in range(NB):
                    if b == 0:
                        h0_phase(0)
                    f.wait(SS, "vv")
                    f.wait(VV, "vv")
                    ln_group_stats([R[:, (b * NT + t) * D:(b * NT + t + 1) * D]
                                    for t in range(NT)], 0, yb[:, 0:1024])
                    for t in range(NT):
                        slot = (b * NT + t) * D
                        f.wait(SS, "t2v")
                        f.inc("a2v", ln_apply(t, R[:, slot:slot + D], h2[:]))
                        f.wait(TT, "a2v")
                        f.inc("v2t", transpose8(h2, x2T, t * 128, 512, 8 * 512))
                        f.wait(TT, "v2t")
                    x2T_done = f.count("v2t")
                    ldc = {}
                    t2a_c = {}
                    for m in range(2):
                        dma(nc.sync, fc1s[:, (m % 2) * D:((m % 2) + 1) * D],
                            bass.AP(fc1bf, m * 128, [[DFF, 128], [128 * DFF, 8], [1, 128]]),
                            q="f1")
                        ldc[m] = f.count("dma_f1")
                    f.wait_count(TT, "v2t", x2T_done)
                    for m in range(32):
                        if b == 0:
                            dma(nc.gpsimd, fc2W[:, m * D:(m + 1) * D],
                                fc2_wT_in[m * 128:(m + 1) * 128, :], q="g")
                            if m == 31:
                                fc2w_ld = f.count("dma_g")
                        f.wait_count(TT, "dma_f1", ldc[m])
                        pq = (P0, P1, P2, P3)[m % 4]
                        f.wait_lag(TT, "a2t", 3)
                        ins = None
                        for c in range(8):
                            ins = TT.matmul(pq[:],
                                            fc1s[:, (m % 2) * D + c * 128:(m % 2) * D + (c + 1) * 128],
                                            x2T[:, c * 512:(c + 1) * 512],
                                            start=(c == 0), stop=(c == 7))
                        f.inc("t2a", ins)
                        t2a_c[m] = f.count("t2a")
                        if m + 2 < 32:
                            f.wait_count(nc.sync, "t2a", t2a_c[m])
                            dma(nc.sync, fc1s[:, ((m + 2) % 2) * D:(((m + 2) % 2) + 1) * D],
                                bass.AP(fc1bf, (m + 2) * 128,
                                        [[DFF, 128], [128 * DFF, 8], [1, 128]]),
                                q="f1")
                            ldc[m + 2] = f.count("dma_f1")
                        f.wait(SS, "t2a")
                        f.inc("a2t", SS.activation(yb[:, m * 512:(m + 1) * 512], pq[:],
                                                   AF.Gelu, bias=bf1_fm[:, m:m + 1]))
                    if b + 1 < NB:
                        h0_phase(b + 1)
                    for t in range(NT):
                        slot = (b * NT + t) * D
                        if t == 0:
                            f.wait(TT, "a2t")
                        if b == 0 and t == 0:
                            f.wait_count(TT, "dma_g", fc2w_ld)
                        fa, fb = ((P4, Psim2), (P0, P1))[t % 2]
                        f.wait_lag(nc.sync, "xs2", 1)
                        dma(nc.sync, xs[:, (t % 2) * D:(t % 2 + 1) * D],
                            x_own[b * NPOS + t * 128:b * NPOS + (t + 1) * 128, :])
                        xld = f.count("dma_ld")
                        f.wait_lag(TT, "v2t", 1)
                        ins = None
                        for (s, pp) in ((0, fa), (1, fb)):
                            for Y in range(32):
                                TT.matmul(pp[:],
                                          yb[:, Y * 512 + t * 128:Y * 512 + (t + 1) * 128],
                                          fc2W[:, Y * D + s * 512:Y * D + (s + 1) * 512],
                                          start=(Y == 0), stop=False)
                            ins = TT.matmul(pp[:], ones_all[64:65, :],
                                            rows1[64:65, s * 512:(s + 1) * 512],
                                            start=False, stop=True)
                        f.inc("t2v", ins)
                        f.wait_count(VV, "dma_ld", xld)
                        f.wait(VV, "t2v")
                        f.inc("xs2", VV.tensor_add(R[:, slot:slot + D],
                                                   R[:, slot:slot + D],
                                                   xs[:, (t % 2) * D:(t % 2 + 1) * D]))
                        f.wait(VV, "xs2")
                        VV.scalar_tensor_tensor(R[:, slot:slot + 512], fa[:], 1.0,
                                                R[:, slot:slot + 512], ALU.mult, ALU.add)
                        ins = VV.scalar_tensor_tensor(R[:, slot + 512:slot + D],
                                                      fb[:, 0:512],
                                                      1.0, R[:, slot + 512:slot + D],
                                                      ALU.mult, ALU.add)
                        f.inc("v2t", ins)
                    bar()

        def emit_ffn(tag, prefetch=None):
            s3 = contextlib.ExitStack()
            with s3:
                def sb3(name, shape, dtype):
                    return s3.enter_context(nc.sbuf_tensor(name, list(shape), dtype))

                Win_sb = sb3("Win_sb" + tag, [128, 8 * 64], BF16)
                W1k_sb = sb3("W1k_sb" + tag, [64, D], BF16)
                W2bT_sb = sb3("W2bT_sb" + tag, [128, 8 * 64], BF16)
                Wout_sb = sb3("Wout_sb" + tag, [64, D], BF16)
                h1 = sb3("h1" + tag, [128, D], BF16)
                h1T = sb3("h1T" + tag, [128, 2 * 8 * 512], BF16)
                s_sb = sb3("s_sb" + tag, [64, 512], BF16)
                hh = sb3("hh" + tag, [128, 8 * 512], BF16)
                g_sb = sb3("g_sb" + tag, [64, 512], BF16)

                dma(nc.gpsimd, Win_sb[:],
                    bass.AP(Win_in, 0, [[64, 128], [8192, 8], [1, 64]]), q="g")
                dma(nc.gpsimd, W1k_sb[:], W1k_in[:], q="g")
                dma(nc.gpsimd, W2bT_sb[:],
                    bass.AP(W2bT_in, 0, [[64, 128], [8192, 8], [1, 64]]), q="g")
                dma(nc.gpsimd, Wout_sb[:], Wout_in[:], q="g")
                if prefetch:
                    prefetch()
                h1T_done = {}

                def ln_ffn(b):
                    p3 = (b % 2) * 4096
                    f.wait(VV, "a2v")
                    f.wait(SS, "t2v")
                    ln_group_stats([R[:, (b * NT + t) * D:(b * NT + t + 1) * D]
                                    for t in range(NT)], (b % 2) * 4, h1[:])
                    cnt = None
                    for t in range(NT):
                        slot = (b * NT + t) * D
                        f.wait(SS, "t2v")
                        f.inc("a2v", ln_apply((b % 2) * 4 + t, R[:, slot:slot + D],
                                              h1[:]))
                        f.wait(TT, "a2v")
                        if cnt is not None:
                            f.wait_count(TT, "v2t", cnt)
                        cnt = f.inc("v2t", transpose8(h1, h1T, p3 + t * 128, 512,
                                                      2 * 8 * 512))
                    h1T_done[b] = cnt

                ln_ffn(0)
                dwait(TT, "g")
                for b in range(NB):
                    p3 = (b % 2) * 4096
                    f.wait(TT, "v2t")
                    f.wait(TT, "a2t")
                    ins = None
                    for c in range(8):
                        ins = TT.matmul(P0[0:64, :], Win_sb[:, c * 64:(c + 1) * 64],
                                        h1T[:, p3 + c * 512:p3 + (c + 1) * 512],
                                        start=(c == 0), stop=(c == 7))
                    f.inc("t2a", ins)
                    f.wait(SS, "t2a")
                    f.inc("a2t", SS.activation(s_sb[:], P0[0:64, :], AF.Identity,
                                               bias=bin_sb[:]))
                    f.wait(TT, "a2t")
                    for hc in range(8):
                        pq = (P1, P2)[hc % 2]
                        if hc:
                            f.wait_lag(TT, "a2t", 1)
                        f.inc("t2a", TT.matmul(pq[:], W1k_sb[:, hc * 128:(hc + 1) * 128],
                                               s_sb[:], start=True, stop=True))
                        f.wait(SS, "t2a")
                        f.inc("a2t", SS.activation(hh[:, hc * 512:(hc + 1) * 512], pq[:],
                                                   AF.Relu, bias=b1fm[:, hc:hc + 1]))
                    if b + 1 < NB:
                        ln_ffn(b + 1)
                    f.wait(TT, "a2t")
                    ins = None
                    for hc in range(8):
                        ins = TT.matmul(P3[0:64, :], W2bT_sb[:, hc * 64:(hc + 1) * 64],
                                        hh[:, hc * 512:(hc + 1) * 512],
                                        start=(hc == 0), stop=(hc == 7))
                    f.inc("t2a", ins)
                    f.wait(SS, "t2a")
                    f.inc("a2t", SS.activation(g_sb[:], P3[0:64, :], AF.Identity,
                                               bias=b2_sb[:]))
                    f.wait(TT, "a2t")
                    for t in range(NT):
                        slot = (b * NT + t) * D
                        f.wait(TT, "v2t")
                        ins = None
                        for (s, pp) in ((0, P0), (1, P1)):
                            TT.matmul(pp[:], g_sb[:, t * 128:(t + 1) * 128],
                                      Wout_sb[:, s * 512:(s + 1) * 512],
                                      start=True, stop=False)
                            ins = TT.matmul(pp[:], ones_all[64:65, :],
                                            rows2[64:65, s * 512:(s + 1) * 512],
                                            start=False, stop=True)
                        f.inc("t2v", ins)
                        f.wait(VV, "t2v")
                        VV.scalar_tensor_tensor(R[:, slot:slot + 512], P0[:], 1.0,
                                                R[:, slot:slot + 512], ALU.mult, ALU.add)
                        ins = VV.scalar_tensor_tensor(R[:, slot + 512:slot + D], P1[:],
                                                      1.0, R[:, slot + 512:slot + D],
                                                      ALU.mult, ALU.add)
                        f.inc("v2t", ins)
                bar()

        s4w = contextlib.ExitStack()
        s4w_prefetch = None
        if upto >= 4:
            Wq_sb = s4w.enter_context(nc.sbuf_tensor("Wq_sb", [128, 8 * D], BF16))
            Wk_sb = s4w.enter_context(nc.sbuf_tensor("Wk_sb", [128, 8 * D], BF16))
            Wv_sb = s4w.enter_context(nc.sbuf_tensor("Wv_sb", [128, 8 * D], BF16))

            def s4w_prefetch():
                for c in range(8):
                    dma(nc.gpsimd, Wq_sb[:, c * D:(c + 1) * D],
                        WqT_in[c * 128:(c + 1) * 128, :], q="s4w")
                    dma(nc.gpsimd, Wk_sb[:, c * D:(c + 1) * D],
                        WkT_in[c * 128:(c + 1) * 128, :], q="s4w")
                    dma(nc.gpsimd, Wv_sb[:, c * D:(c + 1) * D],
                        WvT_in[c * 128:(c + 1) * 128, :], q="s4w")

        if upto >= 3:
            emit_ffn("a", prefetch=s4w_prefetch)

        if upto >= 4:
            s4 = contextlib.ExitStack()
            with s4:
                def sb4(name, shape, dtype):
                    return s4.enter_context(nc.sbuf_tensor(name, list(shape), dtype))

                h2b = sb4("h2b", [128, D], BF16)
                h2T = sb4("h2T", [128, 4 * 1024], BF16)
                bvbc = sb4("bvbc", [128, D], F32)
                q_sb = sb4("q_sb", [128, 2 * NB * D], BF16)
                k_sb = sb4("k_sb", [128, 2 * NB * D], BF16)
                v_sb4 = sb4("v_sb4", [128, 2 * NB * D], BF16)
                scrS = sb4("scrS", [128, D], BF16)
                qk_sb = sb4("qk_sb", [128, 24], F32)
                dd = sb4("dd", [128, 16], F32)
                mm4 = sb4("mm4", [128, 4], F32)
                ex4 = sb4("ex4", [128, 16], F32)
                sm4 = sb4("sm4", [128, 8], F32)
                sim4 = sb4("sim4", [128, 16], F32)
                osum = sb4("osum", [128, NB * D], F32)

                dma(nc.gpsimd, bvbc[:], bv_in.ap().broadcast_to([128, D]), q="g")
                bar()

                mixc = {}
                h2T_done = {}

                def s4_stats(pt):
                    ln_group_stats([R[:, (b_ * NT + pt) * D:(b_ * NT + pt + 1) * D]
                                    for b_ in range(NB)], (pt % 2) * 4, scrS[:])

                def s4_trans(pt):
                    for b in range(NB):
                        slot = (b * NT + pt) * D
                        f.wait(SS, "t2v")
                        f.inc("a2v", ln_apply((pt % 2) * 4 + b, R[:, slot:slot + D],
                                              h2b[:]))
                        f.wait(TT, "a2v")
                        f.inc("v2t", transpose8(h2b, h2T, b * 1024, 128, 4 * 1024))
                        f.wait(TT, "v2t")
                    h2T_done[pt] = f.count("v2t")

                def s4_qkv(pt):
                    p = (pt % 2) * NB * D
                    if pt == 0:
                        dwait(TT, "s4w")
                    f.wait_count(TT, "v2t", h2T_done[pt])
                    if pt >= 2:
                        f.wait_count(SS, "vv", mixc[pt - 2])
                    for b in range(NB):
                        f.wait_lag(TT, "a2t", 1)
                        for c in range(8):
                            for (s_, pp, W_) in ((0, P0, Wq_sb), (1, P1, Wq_sb),
                                                 (0, P2, Wk_sb), (1, P3, Wk_sb)):
                                TT.matmul(pp[:], h2T[:, b * 1024 + c * 128:b * 1024 + (c + 1) * 128],
                                          W_[:, c * D + s_ * 512:c * D + (s_ + 1) * 512],
                                          start=(c == 0), stop=False)
                        TT.matmul(P0[:], ones_all[0:1, :], rows2[0:1, 0:512],
                                  start=False, stop=True)
                        TT.matmul(P1[:], ones_all[0:1, :], rows2[0:1, 512:D],
                                  start=False, stop=True)
                        TT.matmul(P2[:], ones_all[32:33, :], rows2[32:33, 0:512],
                                  start=False, stop=True)
                        ins = TT.matmul(P3[:], ones_all[32:33, :], rows2[32:33, 512:D],
                                        start=False, stop=True)
                        f.inc("t2a", ins)
                        f.wait(SS, "t2a")
                        SS.activation(q_sb[:, p + b * D:p + b * D + 512], P0[:],
                                      AF.Identity)
                        SS.activation(q_sb[:, p + b * D + 512:p + (b + 1) * D], P1[:],
                                      AF.Identity)
                        SS.activation(k_sb[:, p + b * D:p + b * D + 512], P2[:],
                                      AF.Identity)
                        f.inc("a2t", SS.activation(k_sb[:, p + b * D + 512:p + (b + 1) * D],
                                                   P3[:], AF.Identity))
                        f.wait(TT, "a2t")
                        ins = None
                        for c in range(8):
                            for (s_, pp) in ((0, P0), (1, P1)):
                                ins = TT.matmul(pp[:], h2T[:, b * 1024 + c * 128:b * 1024 + (c + 1) * 128],
                                                Wv_sb[:, c * D + s_ * 512:c * D + (s_ + 1) * 512],
                                                start=(c == 0), stop=(c == 7))
                        f.inc("t2a", ins)
                        f.wait(SS, "t2a")
                        SS.activation(v_sb4[:, p + b * D:p + b * D + 512], P0[:],
                                      AF.Identity)
                        f.inc("a2t", SS.activation(
                            v_sb4[:, p + b * D + 512:p + (b + 1) * D], P1[:],
                            AF.Identity))
                        f.wait(TT, "a2t")

                def s4_mix(pt):
                    p = (pt % 2) * NB * D
                    g16 = scrA[:].bitcast(BF16)[:, 0:D]
                    if pt == 0:
                        dwait(VV, "g")
                    f.wait(VV, "a2t")
                    for t in range(NB):
                        for j in range(NB):
                            f.inc("vv", VV.scalar_tensor_tensor(
                                g16, q_sb[:, p + t * D:p + (t + 1) * D], 1.0,
                                k_sb[:, p + j * D:p + (j + 1) * D], ALU.mult, ALU.mult,
                                accum_out=qk_sb[:, t * 4 + j:t * 4 + j + 1]))
                    for t in range(NB):
                        f.inc("vv", VV.scalar_tensor_tensor(
                            g16, q_sb[:, p + t * D:p + (t + 1) * D], 1.0,
                            q_sb[:, p + t * D:p + (t + 1) * D], ALU.mult, ALU.mult,
                            accum_out=qk_sb[:, 16 + t:17 + t]))
                        f.inc("vv", VV.scalar_tensor_tensor(
                            g16, k_sb[:, p + t * D:p + (t + 1) * D], 1.0,
                            k_sb[:, p + t * D:p + (t + 1) * D], ALU.mult, ALU.mult,
                            accum_out=qk_sb[:, 20 + t:21 + t]))
                    f.wait(VV, "vv")
                    q2e = bass.AP(qk_sb, 16, [[24, 128], [1, 4], [0, 4]])
                    k2e = bass.AP(qk_sb, 20, [[24, 128], [0, 4], [1, 4]])
                    f.inc("vv", VV.scalar_tensor_tensor(dd[:], qk_sb[:, 0:16], -2.0,
                                                        q2e, ALU.mult, ALU.add))
                    f.wait(VV, "vv")
                    f.inc("vv", VV.tensor_add(dd[:], dd[:], k2e))
                    f.wait(VV, "vv")
                    f.inc("vv", VV.tensor_scalar_max(dd[:], dd[:], 0.0))
                    f.wait(VV, "vv")
                    f.inc("vv", VV.tensor_reduce(
                        mm4[:], dd.ap().rearrange("p (t j) -> p t j", j=4),
                        AX.X, ALU.min))
                    f.wait(VV, "vv")
                    mne = bass.AP(mm4, 0, [[4, 128], [1, 4], [0, 4]])
                    f.inc("v2a", VV.tensor_sub(dd[:], dd[:], mne))
                    f.wait(SS, "v2a")
                    f.inc("a2v", SS.activation(ex4[:], dd[:], AF.Exp, scale=-0.5))
                    f.wait(VV, "a2v")
                    f.inc("vv", VV.tensor_reduce(
                        sm4[:, 0:4], ex4.ap().rearrange("p (t j) -> p t j", j=4),
                        AX.X, ALU.add))
                    f.wait(VV, "vv")
                    f.inc("vv", VV.reciprocal(sm4[:, 4:8], sm4[:, 0:4]))
                    f.wait(VV, "vv")
                    rce = bass.AP(sm4, 4, [[8, 128], [1, 4], [0, 4]])
                    f.inc("vv", VV.tensor_mul(sim4[:], ex4[:], rce))
                    f.wait(VV, "vv")
                    for j in range(NB):
                        for t in range(NB):
                            f.inc("vv", VV.scalar_tensor_tensor(
                                osum[:, t * D:(t + 1) * D],
                                v_sb4[:, p + j * D:p + (j + 1) * D],
                                sim4[:, t * 4 + j:t * 4 + j + 1],
                                bvbc[:] if j == 0 else osum[:, t * D:(t + 1) * D],
                                ALU.mult, ALU.add))
                        f.wait(VV, "vv")
                    for t in range(NB):
                        slot = (t * NT + pt) * D
                        f.inc("vv", VV.tensor_add(R[:, slot:slot + D],
                                                  R[:, slot:slot + D],
                                                  osum[:, t * D:(t + 1) * D]))
                    mixc[pt] = f.count("vv")

                s4_stats(0)
                s4_trans(0)
                for pt in range(NT):
                    s4_qkv(pt)
                    if pt + 1 < NT:
                        s4_stats(pt + 1)
                        s4_trans(pt + 1)
                    s4_mix(pt)
                bar()
        s4w.close()

        if upto >= 5:
            emit_ffn("b")

        if upto == 11:
            return nc
        bar()
        for b in range(NB):
            for t in range(NT):
                slot = (b * NT + t) * D
                dma(nc.sync, out[b * NPOS + t * 128:b * NPOS + (t + 1) * 128, :],
                    R[:, slot:slot + D], q="st")
        dwait(nc.sync, "st")
        bar()
    return nc


def _prep_weights(inputs):
    g = {k: np.asarray(v, np.float32) for k, v in inputs.items()}
    fm = lambda v, k: np.ascontiguousarray(v.reshape(k, 128).T)
    row = lambda v: np.ascontiguousarray(v.reshape(1, -1))
    W = {}
    W["qkv_wT"] = np.ascontiguousarray(g["L0_attn_a_qkv_w"].T)
    W["ln1_w_fm"] = fm(g["L0_attn_a_ln1_w"], 8)
    W["ln1_b_fm"] = fm(g["L0_attn_a_ln1_b"], 8)
    W["qkv_b_row"] = row(g["L0_attn_a_qkv_b"])
    W["proj_wT"] = np.ascontiguousarray(g["L0_attn_a_proj_w"].T)
    W["proj_b_row"] = row(g["L0_attn_a_proj_b"])
    W["ln2_w_fm"] = fm(g["L0_attn_a_ln2_w"], 8)
    W["ln2_b_fm"] = fm(g["L0_attn_a_ln2_b"], 8)
    W["fc1_wT"] = np.ascontiguousarray(g["L0_attn_a_fc1_w"].T)
    W["fc1_b_row"] = row(g["L0_attn_a_fc1_b"])
    W["fc2_wT"] = np.ascontiguousarray(g["L0_attn_a_fc2_w"].T)
    W["fc2_b_row"] = row(g["L0_attn_a_fc2_b"])
    W["WqT"] = np.ascontiguousarray(g["L0_attn_b_Wq"].T)
    W["WkT"] = np.ascontiguousarray(g["L0_attn_b_Wk"].T)
    W["WvT"] = np.ascontiguousarray(g["L0_attn_b_Wv"].T)
    W["bq_row"] = row(g["L0_attn_b_bq"])
    W["bk_row"] = row(g["L0_attn_b_bk"])
    W["bv_row"] = row(g["L0_attn_b_bv"])
    W["Win"] = np.ascontiguousarray(g["L0_ffn_Win"])
    W["bin_fm"] = np.ascontiguousarray(g["L0_ffn_bin_"].reshape(64, 1))
    W1 = g["L0_ffn_W1"]
    W1k = np.zeros((64, D), np.float32)
    W1k[np.arange(64)[:, None],
        np.arange(64)[:, None] * 16 + np.arange(16)[None, :]] = W1
    W["W1kron"] = W1k
    W["b1_fm"] = fm(g["L0_ffn_b1"].reshape(-1), 8)
    W2 = g["L0_ffn_W2"]
    W2b = np.zeros((D, 64), np.float32)
    W2b[np.arange(64)[:, None] * 16 + np.arange(16)[None, :],
        np.arange(64)[:, None]] = W2
    W["W2blkT"] = W2b
    W["b2_fm"] = np.ascontiguousarray(g["L0_ffn_b2"].reshape(64, 1))
    W["Wout"] = np.ascontiguousarray(g["L0_ffn_Wout"])
    W["bout_row"] = row(g["L0_ffn_bout"])
    return W


def _host_inputs(inputs, core, weights):
    x = np.asarray(inputs["L0_attn_a_x"], np.float32)
    c0 = core * NPOS
    m = {"x_own": np.ascontiguousarray(x[:, c0:c0 + NPOS, :].reshape(NB * NPOS, D))}
    if core == 0:
        m["x_hl"] = np.zeros((NB * 128, D), np.float32)
    else:
        m["x_hl"] = np.ascontiguousarray(x[:, c0 - 128:c0, :].reshape(NB * 128, D))
    if core == NCORES - 1:
        m["x_hr"] = np.zeros((NB * 128, D), np.float32)
    else:
        m["x_hr"] = np.ascontiguousarray(
            x[:, c0 + NPOS:c0 + NPOS + 128, :].reshape(NB * 128, D))
    maskv = np.ones((128, 6), np.float32)
    if core == 0:
        maskv[:, 0] = 0.0
    if core == NCORES - 1:
        maskv[:, 5] = 0.0
    m["maskv"] = maskv
    m.update(weights)
    return m


def kernel(_upto=5, **inputs):
    weights = _prep_weights(inputs)
    nc = build_program(upto=_upto)
    in_maps = [_host_inputs(inputs, c, weights) for c in range(NCORES)]
    res = run_bass_kernel_spmd(nc, in_maps, core_ids=list(range(NCORES)))
    outs = [res.results[c]["out"].reshape(NB, NPOS, D) for c in range(NCORES)]
    return np.ascontiguousarray(np.concatenate(outs, axis=1))
